# version 22
# baseline (speedup 1.0000x reference)
import contextlib
import numpy as np
import concourse.bass as bass
import concourse.mybir as mybir
import concourse.bass_utils as _bass_utils
from concourse.bass_utils import run_bass_kernel_spmd

if not getattr(_bass_utils, "_ldw_opt_patched", False):
    _orig_run_command = _bass_utils.run_command

    def _run_command_ldw(cmd, **kw):
        cmd = [("--enable-ldw-opt=true" if c == "--enable-ldw-opt=false" else c)
               for c in cmd]
        return _orig_run_command(cmd, **kw)

    _bass_utils.run_command = _run_command_ldw
    _bass_utils._ldw_opt_patched = True

F32 = mybir.dt.float32
BF16 = mybir.dt.bfloat16
FP16 = mybir.dt.float16
FP8 = mybir.dt.float8e4
AF = mybir.ActivationFunctionType
ALU = mybir.AluOpType
AX = mybir.AxisListType

L = 4096
D = 1024
CH = 8
NT = 32
HC = 32
EPS = 1e-6
LNEPS = 1e-5
ENGINES = ("tensor", "vector", "scalar", "gpsimd", "sync")
SEM_OF = {"tensor": "T", "vector": "V", "scalar": "S", "gpsimd": "G", "sync": "Y"}


class Em:

    POOL = {"T": 24, "V": 24, "S": 24, "G": 6, "Y": 6, "DG": 8, "DY": 8}

    def __init__(self, nc):
        self.nc = nc
        self.prog = {e: [] for e in ENGINES}
        self.sems = {}
        self.cnt = {}
        self.seq = {}
        self.waited = {}

    def make_sems(self, stack):
        for ch, n in self.POOL.items():
            for i in range(n):
                s = f"{ch}{i}"
                self.sems[s] = stack.enter_context(self.nc.semaphore(s))
                self.cnt[s] = 0

    def _next(self, ch, by):
        idx = self.seq.get(ch, 0)
        self.seq[ch] = idx + 1
        s = f"{ch}{idx % self.POOL[ch]}"
        self.cnt[s] += by
        return s, self.cnt[s]

    def op(self, eng, fn, inc=None, drain=True):
        if inc is None:
            if not drain:
                def run0(e, fn=fn):
                    fn(e)
                    e._skip_drain = True
                self.prog[eng].append(run0)
            else:
                self.prog[eng].append(fn)
            return None
        s, val = self._next(inc, 1)
        sems = self.sems

        if drain:
            def run(e, fn=fn, s=s):
                fn(e).then_inc(sems[s], 1)
        else:
            def run(e, fn=fn, s=s):
                fn(e).then_inc(sems[s], 1)
                e._skip_drain = True
        self.prog[eng].append(run)
        return (s, val)

    def dma(self, eng, out, in_, ch):
        s, val = self._next(ch, 16)
        if val > 16:
            self.wait(eng, (s, val - 16))
        sems = self.sems

        def run(e, out=out, in_=in_, s=s):
            e.dma_start(out, in_).then_inc(sems[s], 16)
        self.prog[eng].append(run)
        return (s, val)

    def wait(self, eng, token):
        if token is None:
            return
        sem, val = token
        if self.waited.get((eng, sem), 0) >= val:
            return
        self.waited[(eng, sem)] = val
        sems = self.sems

        def run(e, sem=sem, val=val):
            e.wait_ge(sems[sem], val)
        self.prog[eng].append(run)

    def wait_many(self, eng, toks):
        for t in toks:
            self.wait(eng, t)

    def wait_final(self, eng, toks):
        best = {}
        for t in toks:
            if t is None:
                continue
            s, v = t
            best[s] = max(best.get(s, 0), v)
        for s, v in best.items():
            self.wait(eng, (s, v))

    def mark(self, eng):
        s, val = self._next(SEM_OF[eng], 1)
        sems = self.sems

        def run(e, s=s):
            e.drain().then_inc(sems[s], 1)
        self.prog[eng].append(run)
        return (s, val)

    def barrier(self):
        toks = {e: self.mark(e) for e in ENGINES}
        for e in ENGINES:
            for o in ENGINES:
                if o != e:
                    self.wait(e, toks[o])


class _StopEmission(Exception):
    pass


def build_program(stop=None, dump=None):
    nc = bass.Bass("TRN2", target_bir_lowering=False)

    x_d = nc.dram_tensor("x", [L, D], F32, kind="ExternalInput").ap()
    wq_d = nc.dram_tensor("wq", [128, 8192], F32, kind="ExternalInput").ap()
    wk_d = nc.dram_tensor("wk", [128, 8192], F32, kind="ExternalInput").ap()
    wv_d = nc.dram_tensor("wv", [128, 8192], F32, kind="ExternalInput").ap()
    wo_d = nc.dram_tensor("wo", [128, 8192], F32, kind="ExternalInput").ap()
    w1_d = nc.dram_tensor("w1", [128, 32768], F32, kind="ExternalInput").ap()
    w2_d = nc.dram_tensor("w2", [128, 32768], F32, kind="ExternalInput").ap()
    bq_d = nc.dram_tensor("bq", [128, 8], F32, kind="ExternalInput").ap()
    bk_d = nc.dram_tensor("bk", [128, 8], F32, kind="ExternalInput").ap()
    b1_d = nc.dram_tensor("b1", [128, 32], F32, kind="ExternalInput").ap()
    bv_d = nc.dram_tensor("bv", [1, D], F32, kind="ExternalInput").ap()
    bo_d = nc.dram_tensor("bo", [1, D], F32, kind="ExternalInput").ap()
    b2_d = nc.dram_tensor("b2", [1, D], F32, kind="ExternalInput").ap()
    y_d = nc.dram_tensor("y", [L, D], F32, kind="ExternalOutput").ap()
    dbg_d = None
    if dump is not None:
        dbg_d = nc.dram_tensor("dbg", [128, 32768], F32, kind="ExternalOutput").ap()

    em = Em(nc)
    stack = contextlib.ExitStack()

    def sbt(name, shape, dtype):
        return stack.enter_context(nc.sbuf_tensor(name, shape, dtype))

    def pst(name, shape, dtype):
        return stack.enter_context(nc.psum_tensor(name, shape, dtype))

    def place(base, name, shape, dtype, off):
        h = nc.virtual_sbuf_tensor(name, shape, dtype)
        nc._tensor(h.name, h.shape, h.dtype, type="SB")
        mloc = nc.lookup_mloc(h)
        mloc.allocated = True
        mloc.addr = nc.lookup_mloc(base).addr + off
        return h

    XT = sbt("XT", [128, 32768], BF16)
    QT = sbt("QT", [128, 32768], BF16)
    KT = sbt("KT", [128, 32768], BF16)
    ST = sbt("ST", [128, 5984], mybir.dt.uint8)
    POOL = sbt("POOL", [128, 10240], mybir.dt.uint8)

    ident = place(ST, "ident", [128, 128], BF16, 0)
    selbc = place(ST, "selbc", [48, 1024], BF16, 256)
    ones_c = place(ST, "ones_c", [128, 1], BF16, 2304)
    ones_rb = place(ST, "ones_rb", [1, 128], BF16, 2336)
    ones_rh = place(ST, "ones_rh", [1, 128], FP16, 2592)
    rowb = place(ST, "rowb", [1, D], BF16, 2848)
    rowh = place(ST, "rowh", [1, D], FP16, 2848)
    bq_sb = place(ST, "bq_sb", [128, 8], F32, 4896)
    bk_sb = place(ST, "bk_sb", [128, 8], F32, 4928)
    b1_sb = place(ST, "b1_sb", [128, 32], F32, 4960)
    css = place(ST, "css", [48, 16], F32, 5088)
    k1q1 = place(ST, "k1q1", [128, 16], F32, 5152)
    acc = place(ST, "acc", [128, 16], F32, 5216)
    stat = place(ST, "stat", [128, 16], F32, 5280)
    bd = place(ST, "bd", [128, 308], BF16, 5344)
    lneps_c = place(ST, "lneps_c", [128, 1], F32, 5960)
    mhalf_c = place(ST, "mhalf_c", [128, 1], F32, 5964)
    mone_c = place(ST, "mone_c", [128, 1], F32, 5968)

    xstg = [place(KT, f"xstg{i}", [128, 2048], BF16, 57344 + i * 4096) for i in range(2)]
    wqs = [place(POOL, f"wqs{i}", [128, 1024], BF16, i * 4096) for i in range(2)]
    wks = [place(POOL, f"wks{i}", [128, 1024], BF16, 2048 + i * 4096) for i in range(2)]
    si_sm = place(POOL, "si_sm", [16, L], BF16, 0)
    so_sm = place(POOL, "so_sm", [48, L], BF16, 0)
    rsc = place(POOL, "rsc", [48, 512], BF16, 8192)
    disc2 = [place(POOL, "disc0", [128, 512], BF16, 9216),
             place(POOL, "disc1", [128, 512], BF16, 8192)]
    sctok = place(POOL, "sctok", [128, 512], BF16, 9216)
    Asb = place(POOL, "Asb", [128, 1024], BF16, 0)
    ATsb = place(POOL, "ATsb", [128, 1024], BF16, 2048)
    wvs = [place(POOL, f"wvs{i}", [128, 1024], BF16, 4096 + i * 2048) for i in range(2)]
    kvsb = place(POOL, "kvsb", [128, 1024], BF16, 8192)
    kscT = place(ST, "kscT", [1, 128], BF16, 5344)
    r1b2 = [place(POOL, f"r1b{i}", [128, 1024], BF16, i * 2048) for i in range(2)]
    sqs = place(POOL, "sqs", [128, 1024], FP8, 4096)
    w1cmb = [place(POOL, f"w1cmb{i}", [128, 1024], BF16, i * 2048) for i in range(2)]
    sqs2 = place(POOL, "sqs2", [128, 1024], FP8, 4096)
    outs = place(POOL, "outs", [128, 1024], F32, 5120)
    f16h = place(POOL, "f16h", [128, 512], FP16, 9216)

    w2q = place(QT, "w2q", [128, 32768], FP8, 0)
    h_sb = place(QT, "h_sb", [128, 32768], FP8, 32768)
    wo_sb = place(QT, "wo_sb", [128, 8192], BF16, 32768)

    pA = pst("pA", [128, 1024], F32)
    pB = pst("pB", [128, 1024], F32)
    pC = pst("pC", [128, 512], F32)
    pCx = pst("pCx", [128, 512], F32)
    pT = pst("pT", [128, 1024], BF16)
    pT2 = pst("pT2", [128, 1024], BF16)

    em.make_sems(stack)
    T, V, S, G, Y = ENGINES

    XT4 = XT[:, :].rearrange("p (kc lt j) -> p kc lt j", kc=CH, lt=NT)
    XTq = XT[:, :].rearrange("p (lt kc j) -> p lt kc j", lt=NT, kc=CH)
    KT4 = KT[:, :].rearrange("p (kc lt j) -> p kc lt j", kc=CH, lt=NT)
    pTr = pT[:, :].rearrange("p (kc j) -> p kc j", kc=CH)
    pT2r = pT2[:, :].rearrange("p (kc j) -> p kc j", kc=CH)

    def ck(name):
        if stop == name:
            raise _StopEmission()

    tens = {}

    def _emit_all():
        tens.update(dict(XT=(XT, 0, 128), QT=(QT, 0, 128), KT=(KT, 0, 128),
                         k1q1=(k1q1, 0, 128), css_lo=(css, 0, 16), css_hi=(css, 32, 48),
                         sctok=(sctok, 0, 128), kscT=(kscT, 0, 1), kvsb=(kvsb, 0, 128),
                         si=(si_sm, 0, 16), so=(so_sm, 32, 48), bd=(bd, 0, 128),
                         stat=(stat, 0, 128), h=(h_sb, 0, 128)))
        def gseq(fn):
            t = em.op(G, fn, inc="G")
            em.wait(G, t)
            return t
        gseq(lambda e: e.memset(ident[:, :], 0.0))
        gseq(lambda e: e.affine_select(
            out=ident[:, :], in_=ident[:, :], compare_op=ALU.not_equal,
            fill=1.0, base=0, pattern=[[-1, 128]], channel_multiplier=1))
        for ro in (0, 32):
            reg = selbc[ro:ro + 16, :]
            gseq(lambda e, reg=reg: e.memset(reg, 1.0))
            gseq(lambda e, reg=reg: e.affine_select(
                out=reg, in_=reg, compare_op=ALU.is_ge, fill=0.0,
                base=0, pattern=[[1, 1024]], channel_multiplier=-64))
            gseq(lambda e, reg=reg: e.affine_select(
                out=reg, in_=reg, compare_op=ALU.is_ge, fill=0.0,
                base=63, pattern=[[-1, 1024]], channel_multiplier=64))
        gseq(lambda e: e.memset(ones_c[:, :], 1.0))
        gseq(lambda e: e.memset(lneps_c[:, :], LNEPS))
        gseq(lambda e: e.memset(mhalf_c[:, :], -0.5))
        gseq(lambda e: e.memset(mone_c[:, :], -1.0))
        em.op(V, lambda e: e.memset(ones_rb[:, :], 1.0))
        em.op(V, lambda e: e.memset(ones_rh[:, :], 1.0))
        em.op(V, lambda e: e.memset(k1q1[:, :], 0.0))
        em.op(V, lambda e: e.memset(css[:, :], 0.0))
        em.op(V, lambda e: e.memset(acc[:, :], 0.0))
        em.op(V, lambda e: e.memset(stat[:, :], 0.0))
        t_statics_v = em.mark(V)
        em.wait(G, t_statics_v)
        t_statics = em.mark(G)

        t_bq = em.dma(Y, bq_sb[:, :], bq_d[:, :], "DY")
        t_bk = em.dma(Y, bk_sb[:, :], bk_d[:, :], "DY")
        t_b1 = em.dma(Y, b1_sb[:, :], b1_d[:, :], "DY")
        t_bv = em.dma(G, rowb[:, :], bv_d[:, :], "DG")

        stg_free = [None, None]
        pt_free = [None, None]
        xT_ready = []
        wpre = {}
        for mb in range(2):
            wpre[mb] = (em.dma(G, wqs[mb][:, :], wq_d[:, mb * 1024:(mb + 1) * 1024], "DG"),
                        em.dma(G, wks[mb][:, :], wk_d[:, mb * 1024:(mb + 1) * 1024], "DG"))
        x_pair = x_d.rearrange("(m two p) d -> m p two d", two=2, p=128)
        for mt in range(NT // 2):
            par = mt % 2
            em.wait(G, stg_free[par])
            t_ld = em.dma(G, xstg[par][:, :].rearrange("p (two d) -> p two d", two=2),
                          x_pair[mt], "DG")
            em.wait(T, t_ld)
            em.wait(T, t_statics_v)
            em.wait(T, t_statics)
            tok = None
            for two in range(2):
                lt = 2 * mt + two
                p2 = lt % 2
                pt = pT if p2 == 0 else pT2
                em.wait(T, pt_free[p2])
                for kc in range(CH):
                    tok = em.op(T, lambda e, kc=kc, two=two, par=par, pt=pt: e.transpose(
                        pt[:, kc * 128:(kc + 1) * 128],
                        xstg[par][:, two * 1024 + kc * 128: two * 1024 + (kc + 1) * 128],
                        ident[:, :]),
                        inc="T" if kc == CH - 1 else None)
                if two == 1:
                    stg_free[par] = tok
                eng = V if p2 == 0 else S
                em.wait(eng, tok)
                ptr = pTr if p2 == 0 else pT2r
                dst = XT4[:, :, lt, :]
                if eng == V:
                    t_ev = em.op(V, lambda e, ptr=ptr, dst=dst: e.tensor_copy(dst, ptr), inc="V", drain=False)
                else:
                    t_ev = em.op(S, lambda e, ptr=ptr, dst=dst: e.copy(dst, ptr), inc="S", drain=False)
                pt_free[p2] = t_ev
                xT_ready.append(t_ev)

        ck("P")
        w_free = [None, None]
        ev_per = {0: [], 1: []}
        for mb in range(CH):
            par = mb % 2
            if mb < 2:
                t_wq, t_wk = wpre[mb]
            else:
                em.wait(G, w_free[par])
                t_wq = em.dma(G, wqs[par][:, :], wq_d[:, mb * 1024:(mb + 1) * 1024], "DG")
                t_wk = em.dma(G, wks[par][:, :], wk_d[:, mb * 1024:(mb + 1) * 1024], "DG")
            last_k = None
            em.wait(S, t_bq)
            em.wait(S, t_bk)
            specs = ((wqs[par], t_wq, QT, pA, bq_sb, 0), (wks[par], t_wk, KT, bk_sb and pB, bk_sb, 8))
            specs = ((wqs[par], t_wq, QT, pA, bq_sb, 0), (wks[par], t_wk, KT, pB, bk_sb, 8))

            def emit_unit(wi, n):
                wsl, wtok, dst, psum, bias_sb, accbase = specs[wi]
                bank = psum[:, (n % 2) * 512:(n % 2) * 512 + 512]
                em.wait(T, wtok)
                idx = len(ev_per[wi])
                if idx >= 2:
                    em.wait(T, ev_per[wi][idx - 2])
                em.wait_final(T, xT_ready[4 * n:4 * n + 4])
                last = None
                for kc in range(CH):
                    last = em.op(T, lambda e, wsl=wsl, kc=kc, n=n, bank=bank: e.matmul(
                        bank, wsl[:, kc * 128:(kc + 1) * 128],
                        XT[:, kc * 4096 + n * 512: kc * 4096 + n * 512 + 512],
                        start=(kc == 0), stop=(kc == CH - 1)),
                        inc="T" if kc == CH - 1 else None)
                em.wait(S, last)
                if dst is KT and mb == CH - 1:
                    em.wait(S, stg_free[0])
                    em.wait(S, stg_free[1])
                t_ev = em.op(S, lambda e, bank=bank, dst=dst, mb=mb, n=n, bias_sb=bias_sb, accbase=accbase: e.activation(
                    dst[:, mb * 4096 + n * 512: mb * 4096 + n * 512 + 512],
                    bank, AF.Sigmoid, bias=bias_sb[:, mb:mb + 1],
                    accum_out=acc[:, accbase + n:accbase + n + 1]), inc="S", drain=False)
                ev_per[wi].append(t_ev)
                return last

            if mb < 2:
                for n in range(8):
                    for wi in range(2):
                        last_k = emit_unit(wi, n)
            else:
                for wi in range(2):
                    for n in range(8):
                        last_k = emit_unit(wi, n)
            for wi in range(2):
                _, _, dst, _, _, accbase = specs[wi]
                em.wait(V, ev_per[wi][-1])
                ccol = (8 + mb) if dst is QT else mb
                t_red = em.op(V, lambda e, accbase=accbase, ccol=ccol: e.reduce_sum(
                    k1q1[:, ccol:ccol + 1], acc[:, accbase:accbase + 8], axis=AX.X), inc="V")
                em.wait(S, t_red)
            w_free[par] = last_k
        em.barrier()

        xtok_ready = []
        for lt in range(NT):
            t = em.dma(G, XT[:, lt * 1024:(lt + 1) * 1024], x_d[lt * 128:(lt + 1) * 128, :], "DG")
            xtok_ready.append(t)

        ck("A")
        def bd_tile(base, mb):
            return bd[:, base + 18 * mb: base + 18 * mb + 16]

        def build_bd(colA, colB):
            em.op(V, lambda e: e.memset(bd[:, :], 0.0))
            for base, col in ((0, colA), (144, colB)):
                ev = bd[0:64, base:base + 160].rearrange("p (a b) -> p a b", a=8, b=20)[:, :, 0:1]
                od = bd[64:128, base + 1:base + 161].rearrange("p (a b) -> p a b", a=8, b=20)[:, :, 0:1]
                em.op(V, lambda e, ev=ev, col=col: e.tensor_scalar_add(
                    ev, k1q1[0:64, col:col + 8].rearrange("p (a b) -> p a b", b=1), EPS))
                em.op(V, lambda e, od=od, col=col: e.tensor_scalar_add(
                    od, k1q1[64:128, col:col + 8].rearrange("p (a b) -> p a b", b=1), EPS))
            return em.mark(V)

        t_bd = build_bd(0, 8)

        em.wait(T, t_bd)
        prev_cev = None
        for cidx, (bdc, ro) in enumerate(((0, 0), (144, 32))):
            em.wait(T, prev_cev)
            last = None
            for mb in range(CH):
                last = em.op(T, lambda e, mb=mb, bdc=bdc, ro=ro: e.matmul(
                    pC[ro:ro + 16, 0:1], bd_tile(bdc, mb),
                    ones_c[:, :],
                    start=(mb == 0), stop=(mb == CH - 1)), inc="T" if mb == CH - 1 else None)
            em.wait(V, last)
            prev_cev = em.op(V, lambda e, ro=ro, cidx=cidx: e.tensor_scalar_mul(
                css[ro:ro + 16, cidx:cidx + 1], pC[ro:ro + 16, 0:1], EPS), inc="V")
        t_c12 = em.mark(V)
        em.wait(T, t_c12)

        prev_rec = {0: None, 1: None}
        for which, (src, bdc, ro, dst) in enumerate(((QT, 0, 0, si_sm), (KT, 144, 32, so_sm))):
            for n in range(8):
                pc = pC if n % 2 == 0 else pCx
                em.wait(T, prev_rec[n % 2])
                last = None
                for mb in range(CH):
                    last = em.op(T, lambda e, mb=mb, n=n, src=src, bdc=bdc, ro=ro, pc=pc: e.matmul(
                        pc[ro:ro + 16, 0:512], bd_tile(bdc, mb),
                        src[:, mb * 4096 + n * 512: mb * 4096 + n * 512 + 512],
                        start=(mb == 0), stop=(mb == CH - 1)), inc="T" if mb == CH - 1 else None)
                em.wait(S, last)
                em.op(S, lambda e, pc=pc, ro=ro, which=which: e.activation(
                    pc[ro:ro + 16, 0:512], pc[ro:ro + 16, 0:512], AF.Ln,
                    bias=css[ro:ro + 16, which:which + 1]))
                t_r = em.op(S, lambda e, dst=dst, pc=pc, ro=ro, n=n: e.activation(
                    dst[ro:ro + 16, n * 512:(n + 1) * 512], pc[ro:ro + 16, 0:512],
                    AF.Exp, scale=mone_c[ro:ro + 16, 0:1]), inc="S")
                prev_rec[n % 2] = t_r
        t_siso = em.mark(S)
        em.wait(T, t_siso)

        ck("B")
        slots = [pC[:, 0:512], pCx[:, 0:512], pA[:, 0:512], pA[:, 512:1024],
                 pB[:, 0:512], pB[:, 512:1024]]
        slot_free = [None] * 6
        i_s = 0
        for which, (src, smrow, ro, outbase) in enumerate(((QT, si_sm, 0, 0), (KT, so_sm, 32, 8))):
            for mb in range(CH):
                last_p = None
                for n in range(8):
                    sl = i_s % 6
                    i_s += 1
                    pc = slots[sl]
                    em.wait(T, slot_free[sl])
                    t_b = em.op(T, lambda e, mb=mb, n=n, smrow=smrow, ro=ro, pc=pc: e.matmul(
                        pc, selbc[ro:ro + 16, mb * 128:(mb + 1) * 128],
                        smrow[ro:ro + 16, n * 512:(n + 1) * 512],
                        start=True, stop=True), inc="T")
                    em.wait(V, t_b)
                    t_p = em.op(V, lambda e, src=src, mb=mb, n=n, pc=pc, dd=disc2[i_s % 2]: e.scalar_tensor_tensor(
                        dd[:, :], src[:, mb * 4096 + n * 512: mb * 4096 + n * 512 + 512],
                        1.0, pc, ALU.mult, ALU.mult,
                        accum_out=acc[:, n:n + 1]), inc="V", drain=False)
                    slot_free[sl] = t_p
                    last_p = t_p
                em.wait(V, last_p)
                em.op(V, lambda e, outbase=outbase, mb=mb: e.reduce_sum(
                    k1q1[:, outbase + mb:outbase + mb + 1], acc[:, 0:8], axis=AX.X), inc="V")
        t_qsikso = em.mark(V)

        em.wait(V, t_qsikso)
        t_bd2 = build_bd(8, 0)
        em.wait(T, t_bd2)
        prev_cev = None
        for cidx, (bdc, ro) in enumerate(((0, 0), (144, 32))):
            em.wait(T, prev_cev)
            last = None
            for mb in range(CH):
                last = em.op(T, lambda e, mb=mb, bdc=bdc, ro=ro: e.matmul(
                    pC[ro:ro + 16, 0:1], bd_tile(bdc, mb),
                    ones_c[:, :],
                    start=(mb == 0), stop=(mb == CH - 1)), inc="T" if mb == CH - 1 else None)
            em.wait(V, last)
            prev_cev = em.op(V, lambda e, ro=ro, cidx=cidx: e.tensor_scalar_mul(
                css[ro:ro + 16, 2 + cidx:3 + cidx], pC[ro:ro + 16, 0:1], EPS), inc="V")
        t_c34 = em.mark(V)
        em.wait(T, t_c34)
        ck("C")

        sink_slots = [pC[:, 0:512], pCx[:, 0:512], pA[:, 0:512]]
        srcq_slots = [pA[:, 512:1024], pB[:, 0:512], pB[:, 512:1024]]
        sink_free = [None] * 3
        srcq_free = [None] * 3
        rsc_free = [None, None]
        prev_clip = None
        for n in range(8):
            sl = n % 3
            pc = sink_slots[sl]
            em.wait(T, sink_free[sl])
            last = None
            for mb in range(CH):
                last = em.op(T, lambda e, mb=mb, n=n, pc=pc: e.matmul(
                    pc[0:16, :], bd_tile(0, mb),
                    QT[:, mb * 4096 + n * 512: mb * 4096 + n * 512 + 512],
                    start=(mb == 0), stop=(mb == CH - 1)), inc="T" if mb == CH - 1 else None)
            em.wait(S, last)
            em.wait(S, rsc_free[n % 2])
            sa_buf = rsc if n % 2 == 0 else disc2[0]
            t_sa = em.op(S, lambda e, pc=pc, sa_buf=sa_buf: e.activation(
                sa_buf[0:16, :], pc[0:16, :], AF.Sigmoid, bias=css[0:16, 2:3]), inc="S")
            em.wait(V, t_sa)
            t_sisa = em.op(V, lambda e, n=n, sa_buf=sa_buf: e.tensor_mul(
                si_sm[0:16, n * 512:(n + 1) * 512], si_sm[0:16, n * 512:(n + 1) * 512],
                sa_buf[0:16, :]), inc="V")
            rsc_free[n % 2] = t_sisa
            sink_free[sl] = t_sisa
            sl2 = n % 3
            pcx = srcq_slots[sl2]
            em.wait(T, srcq_free[sl2])
            last2 = None
            for mb in range(CH):
                last2 = em.op(T, lambda e, mb=mb, n=n, pcx=pcx: e.matmul(
                    pcx[32:48, :], bd_tile(144, mb),
                    KT[:, mb * 4096 + n * 512: mb * 4096 + n * 512 + 512],
                    start=(mb == 0), stop=(mb == CH - 1)), inc="T" if mb == CH - 1 else None)
            em.wait(V, last2)
            em.op(V, lambda e, pcx=pcx, n=n: e.tensor_scalar(
                so_sm[32:48, n * 512:(n + 1) * 512], pcx[32:48, :],
                css[32:48, 3:4], 1.0, ALU.add, ALU.min))
            t_clip = em.op(V, lambda e, n=n: e.tensor_scalar_max(
                so_sm[32:48, n * 512:(n + 1) * 512],
                so_sm[32:48, n * 512:(n + 1) * 512], -1.0), inc="V")
            srcq_free[sl2] = t_clip
            prev_clip = t_clip

        em.wait(S, prev_clip)
        t_exp = em.op(S, lambda e: e.activation(
            so_sm[32:48, :], so_sm[32:48, :], AF.Exp,
            accum_out=css[32:48, 8:9]), inc="S")
        em.wait(V, t_exp)
        em.op(V, lambda e: e.reciprocal(css[32:48, 9:10], css[32:48, 8:9]))
        t_sc = em.op(V, lambda e: e.tensor_scalar(
            so_sm[32:48, :], so_sm[32:48, :], css[32:48, 9:10], float(L),
            ALU.mult, ALU.mult), inc="V")

        em.wait(T, t_sc)
        prev_ev = {0: None, 1: None}
        for lt in range(NT):
            par = lt % 2
            pt = pT if par == 0 else pT2
            em.wait(T, prev_ev[par])
            t_tr = em.op(T, lambda e, lt=lt, pt=pt: e.transpose(
                pt[:, 0:16], so_sm[32:48, lt * 128:(lt + 1) * 128],
                ident[32:48, 32:48]), inc="T")
            em.wait(V, t_tr)
            t_ev = em.op(V, lambda e, lt=lt, pt=pt: e.tensor_copy(
                sctok[:, lt * 16:(lt + 1) * 16], pt[:, 0:16]), inc="V")
            prev_ev[par] = t_ev

        ck("D")
        sctok_r = sctok[:, :].rearrange("p (lt h) -> p lt h", lt=NT)
        for mb in range(CH):
            for lt4 in range(NT // 4):
                i = mb * (NT // 4) + lt4
                par = i % 2
                pt = pT if par == 0 else pT2
                lt = lt4 * 4
                em.wait(T, prev_ev[par])
                t_tr = None
                for j in range(4):
                    t_tr = em.op(T, lambda e, mb=mb, lt=lt, j=j, pt=pt: e.transpose(
                        pt[:, j * 128:(j + 1) * 128],
                        KT[:, mb * 4096 + (lt + j) * 128: mb * 4096 + (lt + j) * 128 + 128],
                        ident[:, :]), inc="T" if j == 3 else None)
                em.wait(V, t_tr)
                scb = sctok_r[:, lt:lt + 4, 2 * mb:2 * mb + 2].rearrange(
                    "p b h -> p b h 1" if False else "p (b one) h -> p b one h", one=1
                )
                scb = sctok_r[:, lt:lt + 4, 2 * mb:2 * mb + 2].unsqueeze(3).broadcast_to([128, 4, 2, 64])
                t_ev = em.op(V, lambda e, mb=mb, lt=lt, pt=pt, scb=scb: e.tensor_tensor(
                    KT[:, mb * 4096 + lt * 128: mb * 4096 + lt * 128 + 512].rearrange("p (b h j) -> p b h j", b=4, h=2),
                    pt[:, 0:512].rearrange("p (b h j) -> p b h j", b=4, h=2), scb, ALU.mult), inc="V", drain=False)
                prev_ev[par] = t_ev

        ck("C3")
        slots = [pC[:, 0:512], pCx[:, 0:512], pA[:, 0:512], pA[:, 512:1024],
                 pB[:, 0:512], pB[:, 512:1024]]
        slot_free = [None] * 6
        i_s = 0
        for mb in range(CH):
            for n in range(8):
                sl = i_s % 6
                i_s += 1
                pc = slots[sl]
                em.wait(T, slot_free[sl])
                t_b = em.op(T, lambda e, mb=mb, n=n, pc=pc: e.matmul(
                    pc, selbc[0:16, mb * 128:(mb + 1) * 128],
                    si_sm[0:16, n * 512:(n + 1) * 512], start=True, stop=True), inc="T")
                em.wait(V, t_b)
                t_e = em.op(V, lambda e, mb=mb, n=n, pc=pc: e.tensor_mul(
                    QT[:, mb * 4096 + n * 512: mb * 4096 + n * 512 + 512],
                    QT[:, mb * 4096 + n * 512: mb * 4096 + n * 512 + 512],
                    pc), inc="V", drain=False)
                slot_free[sl] = t_e
        em.barrier()

        ck("E")
        em.wait_final(T, xtok_ready)
        t_kvz = em.op(G, lambda e: e.memset(kvsb[:, :], 0.0), inc="G")
        fstate = dict(t_wv_free={}, pa_free=[None, None], prev_kr=None,
                      prev_AT=None, prev_kv_mm=None, prev_kvev=None,
                      wv_tok={}, amm_tok={})

        def emit_A_mms(c):
            par = c % 2
            pa = pA if par == 0 else pB
            em.wait(G, fstate["t_wv_free"].get(c % 2))
            pf = fstate["pa_free"][c % 2]
            fstate["wv_tok"][c] = em.dma(G, wvs[c % 2][:, :], wv_d[:, c * 1024:(c + 1) * 1024], "DG")
            em.wait_many(T, pf if isinstance(pf, list) else [pf])
            last = None
            for lt in range(NT):
                for half in range(2):
                    last = em.op(T, lambda e, c=c, lt=lt, half=half, pa=pa: e.matmul(
                        pa[:, half * 512: half * 512 + 512],
                        KT[:, c * 4096 + lt * 128: c * 4096 + lt * 128 + 128],
                        XT[:, lt * 1024 + half * 512: lt * 1024 + half * 512 + 512],
                        start=(lt == 0), stop=(lt == NT - 1)),
                        inc="T" if (lt == NT - 1 and half == 1) else None)
            fstate["amm_tok"][c] = last

        def emit_F_tail(c):
            par = c % 2
            pa = pA if par == 0 else pB
            last = fstate["amm_tok"][c]
            prow = pCx if c < 4 else pC
            em.wait(T, fstate["prev_kr"])
            lastr = None
            for lt in range(NT):
                lastr = em.op(T, lambda e, c=c, lt=lt, prow=prow: e.matmul(
                    prow[0:1, 128:256], ones_c[:, :],
                    KT[:, c * 4096 + lt * 128: c * 4096 + lt * 128 + 128],
                    start=(lt == 0), stop=(lt == NT - 1)), inc="T" if lt == NT - 1 else None)
            em.wait(V, lastr)
            em.wait(V, fstate["prev_kv_mm"])
            t_kr = em.op(V, lambda e, prow=prow: e.tensor_copy(
                kscT[0:1, :], prow[0:1, 128:256]), inc="V")
            fstate["prev_kr"] = t_kr
            em.wait(V, last)
            em.wait(V, fstate["prev_AT"])
            t_A = em.op(V, lambda e, pa=pa: e.tensor_copy(Asb[:, :], pa[:, :]), inc="V")
            fstate["pa_free"][par] = t_A
            em.wait(T, t_A)
            em.wait(V, fstate["prev_kv_mm"])
            tr_toks = []
            for jc in range(CH):
                par2 = jc % 2
                pt = pT if par2 == 0 else pT2
                if jc >= 2:
                    em.wait(T, tr_toks[jc - 2])
                t_tr = em.op(T, lambda e, jc=jc, pt=pt: e.transpose(
                    pt[:, 0:128], Asb[:, jc * 128:(jc + 1) * 128], ident[:, :]), inc="T")
                em.wait(V, t_tr)
                t_evt = em.op(V, lambda e, jc=jc, pt=pt: e.tensor_copy(
                    ATsb[:, jc * 128:(jc + 1) * 128], pt[:, 0:128]), inc="V", drain=False)
                tr_toks.append(t_evt)
            fstate["prev_AT"] = tr_toks[-1]
            pkv = pC if c < 4 else pCx
            em.wait_many(T, [tr_toks[-1], fstate["wv_tok"][c], t_kr, fstate["prev_kvev"], t_bv])
            for jc in range(CH):
                em.op(T, lambda e, c=c, jc=jc, pkv=pkv: e.matmul(
                    pkv[:, (c % 4) * 128:(c % 4) * 128 + 128],
                    ATsb[:, jc * 128:(jc + 1) * 128],
                    wvs[c % 2][:, jc * 128:(jc + 1) * 128],
                    start=(jc == 0), stop=False))
            t_kv = em.op(T, lambda e, c=c, pkv=pkv: e.matmul(
                pkv[:, (c % 4) * 128:(c % 4) * 128 + 128],
                kscT[0:1, :],
                rowb[0:1, c * 128:(c + 1) * 128],
                start=False, stop=True), inc="T")
            fstate["t_wv_free"][c % 2] = t_kv
            fstate["prev_kv_mm"] = t_kv
            em.wait(V, t_kvz)
            em.wait(V, t_kv)
            em.op(V, lambda e, c=c, pkv=pkv: e.tensor_copy(
                kvsb[0:64, c * 128: c * 128 + 64],
                pkv[0:64, (c % 4) * 128:(c % 4) * 128 + 64]))
            t_kvev = em.op(V, lambda e, c=c, pkv=pkv: e.tensor_copy(
                kvsb[64:128, c * 128 + 64: c * 128 + 128],
                pkv[64:128, (c % 4) * 128 + 64:(c % 4) * 128 + 128]), inc="V")
            fstate["prev_kvev"] = t_kvev

        wo_w2 = {}

        def emit_G(c):
            par = c % 2
            pa = pA if par == 0 else pB
            slots = [pa[:, 0:512], pa[:, 512:1024]]
            pf = fstate["pa_free"][par]
            pf = pf if isinstance(pf, list) else [pf, pf]
            free = [pf[0], pf[1]]
            for n in range(8):
                sl = n % 2
                em.wait(T, free[sl])
                em.wait(T, fstate["prev_kvev"])
                t_mm = em.op(T, lambda e, c=c, n=n, sl=sl, slots=slots: e.matmul(
                    slots[sl], kvsb[:, c * 128:(c + 1) * 128],
                    QT[:, c * 4096 + n * 512: c * 4096 + n * 512 + 512],
                    start=True, stop=True), inc="T")
                eng = V if n % 2 == 0 else S
                em.wait(eng, t_mm)
                if eng == V:
                    t_ev = em.op(V, lambda e, c=c, n=n, sl=sl, slots=slots: e.tensor_copy(
                        KT[:, c * 4096 + n * 512: c * 4096 + n * 512 + 512], slots[sl]), inc="V", drain=False)
                else:
                    t_ev = em.op(S, lambda e, c=c, n=n, sl=sl, slots=slots: e.copy(
                        KT[:, c * 4096 + n * 512: c * 4096 + n * 512 + 512], slots[sl]), inc="S", drain=False)
                free[sl] = t_ev
            fstate["pa_free"][par] = [free[0], free[1]]
            if c == 3:
                em.wait(G, free[0])
                em.wait(G, free[1])
                wo_w2["w2"] = em.dma(G, w2q[:, :], w2_d[:, :], "DG")
            if c == 5:
                em.wait(G, free[0])
                em.wait(G, free[1])
                wo_w2["wo"] = em.dma(G, wo_sb[:, :], wo_d[:, :], "DG")

        emit_A_mms(0)
        for c in range(1, CH):
            emit_A_mms(c)
            emit_F_tail(c - 1)
            emit_G(c - 1)
        emit_F_tail(CH - 1)
        emit_G(CH - 1)
        em.barrier()

        t_bo = em.dma(G, rowb[:, :], bo_d[:, :], "DG")

        ck("F")

        ck("G")
        em.wait(T, wo_w2["wo"])
        em.wait(T, t_bo)
        pa_free = [None, None]
        pt_free = [None, None]
        prev_norm = None
        r1_free = [None, None]
        pend_tr = []

        def emit_y1_transpose(lt, t_y1):
            par = lt % 2
            em.wait(T, t_y1)
            em.wait(T, pt_free[par])
            pt = pT if par == 0 else pT2
            tok = None
            for kc in range(CH):
                tok = em.op(T, lambda e, kc=kc, lt=lt, pt=pt: e.transpose(
                    pt[:, kc * 128:(kc + 1) * 128],
                    KT[:, kc * 4096 + lt * 128: kc * 4096 + lt * 128 + 128],
                    ident[:, :]), inc="T" if kc == CH - 1 else None)
            eng = V if par == 0 else S
            em.wait(eng, tok)
            ptr = pTr if par == 0 else pT2r
            dstT = XT[:, lt * 1024:(lt + 1) * 1024].rearrange("p (kc j) -> p kc j", kc=CH)
            if eng == V:
                t_evT = em.op(V, lambda e, ptr=ptr, dstT=dstT: e.tensor_copy(dstT, ptr), inc="V", drain=False)
            else:
                t_evT = em.op(S, lambda e, ptr=ptr, dstT=dstT: e.copy(dstT, ptr), inc="S", drain=False)
            pt_free[par] = t_evT

        for lt in range(NT):
            par = lt % 2
            pa = pA if par == 0 else pB
            em.wait(T, pa_free[par])
            last = None
            for c in range(CH):
                for half in range(2):
                    last = em.op(T, lambda e, c=c, lt=lt, half=half, pa=pa: e.matmul(
                        pa[:, half * 512: half * 512 + 512],
                        KT[:, c * 4096 + lt * 128: c * 4096 + lt * 128 + 128],
                        wo_sb[:, c * 1024 + half * 512: c * 1024 + half * 512 + 512],
                        start=(c == 0), stop=False),
                        inc=None)
            for half in range(2):
                last = em.op(T, lambda e, half=half, pa=pa: e.matmul(
                    pa[:, half * 512: half * 512 + 512],
                    ones_rb[:, :], rowb[0:1, half * 512: half * 512 + 512],
                    start=False, stop=True), inc="T" if half == 1 else None)
            em.wait(V, last)
            em.wait(V, r1_free[par])
            r1b = r1b2[par]
            t_r1 = em.op(V, lambda e, pa=pa, lt=lt, r1b=r1b: e.scalar_tensor_tensor(
                r1b[:, :], pa[:, :], 1.0, XT[:, lt * 1024:(lt + 1) * 1024],
                ALU.mult, ALU.add, accum_out=stat[:, 0:1]), inc="V")
            pa_free[par] = t_r1
            em.wait(S, t_r1)
            em.wait(S, prev_norm)
            t_sq = em.op(S, lambda e, r1b=r1b: e.activation(
                sqs[:, :], r1b[:, :], AF.Square, accum_out=stat[:, 1:2]), inc="S")
            em.wait(V, t_sq)
            em.op(V, lambda e: e.tensor_scalar_mul(stat[:, 2:3], stat[:, 0:1], 1.0 / D))
            em.op(V, lambda e: e.tensor_mul(stat[:, 4:5], stat[:, 2:3], stat[:, 2:3]))
            t_var = em.op(V, lambda e: e.scalar_tensor_tensor(
                stat[:, 3:4], stat[:, 1:2], 1.0 / D, stat[:, 4:5],
                ALU.mult, ALU.subtract), inc="V")
            em.wait(S, t_var)
            em.op(S, lambda e: e.activation(stat[:, 5:6], stat[:, 3:4], AF.Ln, bias=lneps_c[:, 0:1]))
            t_rstd = em.op(S, lambda e: e.activation(stat[:, 6:7], stat[:, 5:6], AF.Exp, scale=mhalf_c[:, 0:1]), inc="S")
            em.wait(V, t_rstd)
            em.wait(V, prev_norm)
            t_nmr = em.op(V, lambda e: e.scalar_tensor_tensor(
                stat[:, 7:8], stat[:, 2:3], -1.0, stat[:, 6:7], ALU.mult, ALU.mult), inc="V")
            em.wait(S, t_nmr)
            t_y1 = em.op(S, lambda e, lt=lt, r1b=r1b: e.activation(
                KT4[:, :, lt, :], r1b[:, :], AF.Identity,
                bias=stat[:, 7:8], scale=stat[:, 6:7]), inc="S")
            prev_norm = t_y1
            r1_free[par] = t_y1
            pend_tr.append((lt, t_y1))
            if len(pend_tr) > 1:
                emit_y1_transpose(*pend_tr.pop(0))
        while pend_tr:
            emit_y1_transpose(*pend_tr.pop(0))
        em.barrier()

        t_b2 = em.dma(G, rowh[:, :], b2_d[:, :], "DG")

        ck("H")
        em.wait(T, wo_w2["w2"])
        em.wait(T, t_b2)
        w1_free = [None, None]
        h_ev = []
        prev_f16 = None
        f16_par = [None, None]
        prev_store = None
        prev_out = None
        y_stores = []
        for g in range(4):
            h_ev = []
            for hc in range(HC):
                par = hc % 2
                pa = pB if par == 0 else pA
                wpar = hc % 2
                em.wait(G, w1_free[wpar])
                t_w1a = em.dma(G, w1cmb[wpar][:, :], w1_d[:, hc * 1024:(hc + 1) * 1024], "DG")
                if hc >= 2:
                    em.wait(T, h_ev[hc - 2])
                else:
                    em.wait(T, f16_par[0] if par == 0 else f16_par[1])
                last = None
                em.wait(T, t_w1a)
                for kc in range(CH):
                    for half in range(2):
                        rhs = XTq[:, g * 8 + half * 4: g * 8 + half * 4 + 4, kc, :]
                        want_inc = (kc == CH - 1 and half == 1)
                        tok = em.op(T, lambda e, kc=kc, half=half, pa=pa, wpar=wpar, rhs=rhs: e.matmul(
                            pa[:, half * 512: half * 512 + 512],
                            w1cmb[wpar][:, kc * 128:(kc + 1) * 128], rhs,
                            start=(kc == 0), stop=(kc == CH - 1)),
                            inc="T" if want_inc else None)
                        if kc == CH - 1 and half == 1:
                            last = tok
                w1_free[wpar] = last
                eng = V if par == 0 else S
                em.wait(eng, last)
                em.wait(eng, t_b1)
                if eng == V:
                    t_h = em.op(V, lambda e, hc=hc, pa=pa: e.tensor_scalar(
                        h_sb[:, hc * 1024:(hc + 1) * 1024], pa[:, :],
                        b1_sb[:, hc:hc + 1], 0.0, ALU.add, ALU.max), inc="V", drain=False)
                else:
                    t_h = em.op(S, lambda e, hc=hc, pa=pa: e.activation(
                        h_sb[:, hc * 1024:(hc + 1) * 1024], pa[:, :], AF.Relu,
                        bias=b1_sb[:, hc:hc + 1]), inc="S", drain=False)
                h_ev.append(t_h)
            em.wait_many(T, h_ev)
            for ls in range(8):
                lt = g * 8 + ls
                par = ls % 2
                pb = pB if par == 0 else pA
                em.wait(T, f16_par[par])
                last = None
                h4 = h_sb[:, :].rearrange("p (hc l) -> p hc l", hc=HC)
                w24 = w2q[:, :].rearrange("p (hc d) -> p hc d", hc=HC)
                for hc2 in range(HC // 2):
                    for half in range(2):
                        last = em.op(T, lambda e, hc2=hc2, ls=ls, half=half, pb=pb, h4=h4, w24=w24: e.matmul(
                            pb[:, half * 512: half * 512 + 512],
                            h4[:, 2 * hc2:2 * hc2 + 2, ls * 128: ls * 128 + 128],
                            w24[:, 2 * hc2:2 * hc2 + 2, half * 512: half * 512 + 512],
                            start=(hc2 == 0), stop=False,
                            perf_mode=mybir.MatmulPerfMode.DoubleRow), inc=None)
                for half in range(2):
                    last = em.op(T, lambda e, half=half, pb=pb: e.matmul(
                        pb[:, half * 512: half * 512 + 512],
                        ones_rh[:, :], rowh[0:1, half * 512: half * 512 + 512],
                        start=False, stop=True), inc="T" if half == 1 else None)
                em.wait(V, last)
                t_r2 = None
                for hf in range(2):
                    t_f16 = em.op(V, lambda e, pb=pb, hf=hf: e.tensor_copy(
                        f16h[:, :], pb[:, hf * 512:(hf + 1) * 512]), inc="V")
                    KT4h = KT4[:, 4 * hf:4 * hf + 4, lt, :]
                    t_r2 = em.op(V, lambda e, lt=lt, hf=hf, KT4h=KT4h: e.scalar_tensor_tensor(
                        KT4h, f16h[:, :].rearrange("p (c j) -> p c j", c=4),
                        1.0, KT4h, ALU.mult, ALU.add,
                        accum_out=acc[:, hf:hf + 1]), inc="V")
                    if hf == 1:
                        prev_f16 = t_r2
                        f16_par[par] = t_r2
                em.wait(V, prev_out)
                t_s1 = em.op(V, lambda e: e.tensor_add(stat[:, 8:9], acc[:, 0:1], acc[:, 1:2]), inc="V")
                em.wait(S, t_s1)
                em.wait(S, t_r2)
                t_sq = em.op(S, lambda e, lt=lt: e.activation(
                    sqs2[:, :], KT4[:, :, lt, :], AF.Square, accum_out=stat[:, 9:10]), inc="S")
                em.wait(V, t_sq)
                em.op(V, lambda e: e.tensor_scalar_mul(stat[:, 10:11], stat[:, 8:9], 1.0 / D))
                em.op(V, lambda e: e.tensor_mul(stat[:, 12:13], stat[:, 10:11], stat[:, 10:11]))
                t_var = em.op(V, lambda e: e.scalar_tensor_tensor(
                    stat[:, 11:12], stat[:, 9:10], 1.0 / D, stat[:, 12:13],
                    ALU.mult, ALU.subtract), inc="V")
                em.wait(S, t_var)
                em.op(S, lambda e: e.activation(stat[:, 13:14], stat[:, 11:12], AF.Ln, bias=lneps_c[:, 0:1]))
                t_rstd = em.op(S, lambda e: e.activation(stat[:, 14:15], stat[:, 13:14], AF.Exp, scale=mhalf_c[:, 0:1]), inc="S")
                em.wait(V, t_rstd)
                t_nmr = em.op(V, lambda e: e.scalar_tensor_tensor(
                    stat[:, 15:16], stat[:, 10:11], -1.0, stat[:, 14:15],
                    ALU.mult, ALU.mult), inc="V")
                em.wait(S, t_nmr)
                em.wait(S, prev_store)
                t_out = em.op(S, lambda e, lt=lt: e.activation(
                    outs[:, :], KT4[:, :, lt, :], AF.Identity,
                    bias=stat[:, 15:16], scale=stat[:, 14:15]), inc="S")
                prev_out = t_out
                em.wait(Y, t_out)
                t_st = em.dma(Y, y_d[lt * 128:(lt + 1) * 128, :], outs[:, :], "DY")
                y_stores.append(t_st)
                prev_store = t_st

        em.wait_final(Y, y_stores)


    try:
        _emit_all()
    except _StopEmission:
        pass
    if dump is not None:
        em.barrier()
        col = 0
        for name in dump:
            t, r0, r1 = tens[name]
            ncols = t.shape[1]
            nb = 1024
            for o in range(0, ncols, nb):
                w = min(nb, ncols - o, 32768 - col)
                if w <= 0:
                    break
                em.dma(G, dbg_d[0:r1 - r0, col:col + w], t[r0:r1, o:o + w], "DG")
                col += w
        for i in range(8):
            em.op(G, lambda e, i=i: e.wait_ge(em.sems[f"DG{i}"], em.cnt[f"DG{i}"]))

    with stack:
        with nc.allow_low_precision(reason="deliberate bf16 intermediates within tolerance"), nc.Block() as block:
            @block.tensor
            def _(e):
                for fn in em.prog["tensor"]:
                    fn(e)

            @block.vector
            def _(e):
                for fn in em.prog["vector"]:
                    e._skip_drain = False
                    fn(e)
                    if not getattr(e, "_skip_drain", False):
                        e.drain()

            @block.scalar
            def _(e):
                for fn in em.prog["scalar"]:
                    e._skip_drain = False
                    fn(e)
                    if not getattr(e, "_skip_drain", False):
                        e.drain()

            @block.gpsimd
            def _(e):
                for fn in em.prog["gpsimd"]:
                    fn(e)

            @block.sync
            def _(e):
                for fn in em.prog["sync"]:
                    fn(e)

    return nc


def _images(inputs):
    Wq = np.asarray(inputs["L0_attn_Wq"], np.float32)
    Wk = np.asarray(inputs["L0_attn_Wk"], np.float32)
    Wv = np.asarray(inputs["L0_attn_Wv"], np.float32)
    Wo = np.asarray(inputs["L0_attn_Wo"], np.float32)
    w1 = np.asarray(inputs["L0_ffn_w1"], np.float32)
    w2 = np.asarray(inputs["L0_ffn_w2"], np.float32)

    def img_mb(W):
        return np.ascontiguousarray(
            W.reshape(8, 128, 8, 128).transpose(1, 2, 0, 3).reshape(128, 8192))

    def img_kc(W):
        return np.ascontiguousarray(
            W.reshape(8, 128, 1024).transpose(1, 0, 2).reshape(128, 8192))

    return dict(
        wq=img_mb(Wq), wk=img_mb(Wk), wv=img_mb(Wv), wo=img_kc(Wo),
        w1=np.ascontiguousarray(
            w1.reshape(32, 128, 8, 128).transpose(3, 0, 2, 1).reshape(128, 32768)),
        w2=np.ascontiguousarray(
            w2.reshape(1024, 32, 128).transpose(2, 1, 0).reshape(128, 32768)),
        bq=np.ascontiguousarray(np.asarray(inputs["L0_attn_bq"], np.float32).reshape(8, 128).T),
        bk=np.ascontiguousarray(np.asarray(inputs["L0_attn_bk"], np.float32).reshape(8, 128).T),
        b1=np.ascontiguousarray(np.asarray(inputs["L0_ffn_b1"], np.float32).reshape(32, 128).T),
        bv=np.ascontiguousarray(np.asarray(inputs["L0_attn_bv"], np.float32)[None, :]),
        bo=np.ascontiguousarray(np.asarray(inputs["L0_attn_bo"], np.float32)[None, :]),
        b2=np.ascontiguousarray(np.asarray(inputs["L0_ffn_b2"], np.float32)[None, :]),
    )


def kernel(**inputs):
    x = np.asarray(inputs["L0_attn_x"], np.float32)
    imgs = _images(inputs)
    nc = build_program()
    in_maps = []
    for c in range(8):
        m = {"x": np.ascontiguousarray(x[c])}
        m.update(imgs)
        in_maps.append(m)
    res = run_bass_kernel_spmd(nc, in_maps, core_ids=list(range(8)))
    return np.stack([res.results[c]["y"] for c in range(8)], axis=0).astype(np.float32)


# version 23
# speedup vs baseline: 1.0587x; 1.0587x over previous
import contextlib
import numpy as np
import concourse.bass as bass
import concourse.mybir as mybir
import concourse.bass_utils as _bass_utils
from concourse.bass_utils import run_bass_kernel_spmd

if not getattr(_bass_utils, "_ldw_opt_patched", False):
    _orig_run_command = _bass_utils.run_command

    def _run_command_ldw(cmd, **kw):
        cmd = [("--enable-ldw-opt=true" if c == "--enable-ldw-opt=false" else c)
               for c in cmd]
        return _orig_run_command(cmd, **kw)

    _bass_utils.run_command = _run_command_ldw
    _bass_utils._ldw_opt_patched = True

F32 = mybir.dt.float32
BF16 = mybir.dt.bfloat16
FP16 = mybir.dt.float16
FP8 = mybir.dt.float8e4
AF = mybir.ActivationFunctionType
ALU = mybir.AluOpType
AX = mybir.AxisListType

L = 4096
D = 1024
CH = 8
NT = 32
HC = 32
EPS = 1e-6
LNEPS = 1e-5
ENGINES = ("tensor", "vector", "scalar", "gpsimd", "sync")
SEM_OF = {"tensor": "T", "vector": "V", "scalar": "S", "gpsimd": "G", "sync": "Y"}


class Em:

    POOL = {"T": 24, "V": 24, "S": 24, "G": 6, "Y": 6, "DG": 8, "DY": 8}

    def __init__(self, nc):
        self.nc = nc
        self.prog = {e: [] for e in ENGINES}
        self.sems = {}
        self.cnt = {}
        self.seq = {}
        self.waited = {}

    def make_sems(self, stack):
        for ch, n in self.POOL.items():
            for i in range(n):
                s = f"{ch}{i}"
                self.sems[s] = stack.enter_context(self.nc.semaphore(s))
                self.cnt[s] = 0

    def _next(self, ch, by):
        idx = self.seq.get(ch, 0)
        self.seq[ch] = idx + 1
        s = f"{ch}{idx % self.POOL[ch]}"
        self.cnt[s] += by
        return s, self.cnt[s]

    def op(self, eng, fn, inc=None, drain=True):
        if inc is None:
            if not drain:
                def run0(e, fn=fn):
                    fn(e)
                    e._skip_drain = True
                self.prog[eng].append(run0)
            else:
                self.prog[eng].append(fn)
            return None
        s, val = self._next(inc, 1)
        sems = self.sems

        if drain:
            def run(e, fn=fn, s=s):
                fn(e).then_inc(sems[s], 1)
        else:
            def run(e, fn=fn, s=s):
                fn(e).then_inc(sems[s], 1)
                e._skip_drain = True
        self.prog[eng].append(run)
        return (s, val)

    def dma(self, eng, out, in_, ch):
        s, val = self._next(ch, 16)
        if val > 16:
            self.wait(eng, (s, val - 16))
        sems = self.sems

        def run(e, out=out, in_=in_, s=s):
            e.dma_start(out, in_).then_inc(sems[s], 16)
        self.prog[eng].append(run)
        return (s, val)

    def wait(self, eng, token):
        if token is None:
            return
        sem, val = token
        if self.waited.get((eng, sem), 0) >= val:
            return
        self.waited[(eng, sem)] = val
        sems = self.sems

        def run(e, sem=sem, val=val):
            e.wait_ge(sems[sem], val)
        self.prog[eng].append(run)

    def wait_many(self, eng, toks):
        for t in toks:
            self.wait(eng, t)

    def wait_final(self, eng, toks):
        best = {}
        for t in toks:
            if t is None:
                continue
            s, v = t
            best[s] = max(best.get(s, 0), v)
        for s, v in best.items():
            self.wait(eng, (s, v))

    def mark(self, eng):
        s, val = self._next(SEM_OF[eng], 1)
        sems = self.sems

        def run(e, s=s):
            e.drain().then_inc(sems[s], 1)
        self.prog[eng].append(run)
        return (s, val)

    def barrier(self):
        toks = {e: self.mark(e) for e in ENGINES}
        for e in ENGINES:
            for o in ENGINES:
                if o != e:
                    self.wait(e, toks[o])


class _StopEmission(Exception):
    pass


def build_program(stop=None, dump=None):
    nc = bass.Bass("TRN2", target_bir_lowering=False)

    x_d = nc.dram_tensor("x", [L, D], F32, kind="ExternalInput").ap()
    wq_d = nc.dram_tensor("wq", [128, 8192], F32, kind="ExternalInput").ap()
    wk_d = nc.dram_tensor("wk", [128, 8192], F32, kind="ExternalInput").ap()
    wv_d = nc.dram_tensor("wv", [128, 8192], F32, kind="ExternalInput").ap()
    wo_d = nc.dram_tensor("wo", [128, 8192], F32, kind="ExternalInput").ap()
    w1_d = nc.dram_tensor("w1", [128, 32768], F32, kind="ExternalInput").ap()
    w2_d = nc.dram_tensor("w2", [128, 32768], F32, kind="ExternalInput").ap()
    bq_d = nc.dram_tensor("bq", [128, 8], F32, kind="ExternalInput").ap()
    bk_d = nc.dram_tensor("bk", [128, 8], F32, kind="ExternalInput").ap()
    b1_d = nc.dram_tensor("b1", [128, 32], F32, kind="ExternalInput").ap()
    bv_d = nc.dram_tensor("bv", [1, D], F32, kind="ExternalInput").ap()
    bo_d = nc.dram_tensor("bo", [1, D], F32, kind="ExternalInput").ap()
    b2_d = nc.dram_tensor("b2", [1, D], F32, kind="ExternalInput").ap()
    y_d = nc.dram_tensor("y", [L, D], F32, kind="ExternalOutput").ap()
    dbg_d = None
    if dump is not None:
        dbg_d = nc.dram_tensor("dbg", [128, 32768], F32, kind="ExternalOutput").ap()

    em = Em(nc)
    stack = contextlib.ExitStack()

    def sbt(name, shape, dtype):
        return stack.enter_context(nc.sbuf_tensor(name, shape, dtype))

    def pst(name, shape, dtype):
        return stack.enter_context(nc.psum_tensor(name, shape, dtype))

    def place(base, name, shape, dtype, off):
        h = nc.virtual_sbuf_tensor(name, shape, dtype)
        nc._tensor(h.name, h.shape, h.dtype, type="SB")
        mloc = nc.lookup_mloc(h)
        mloc.allocated = True
        mloc.addr = nc.lookup_mloc(base).addr + off
        return h

    XT = sbt("XT", [128, 32768], BF16)
    QT = sbt("QT", [128, 32768], BF16)
    KT = sbt("KT", [128, 32768], BF16)
    ST = sbt("ST", [128, 5984], mybir.dt.uint8)
    POOL = sbt("POOL", [128, 10240], mybir.dt.uint8)

    ident = place(ST, "ident", [128, 128], BF16, 0)
    selbc = place(ST, "selbc", [48, 1024], BF16, 256)
    ones_c = place(ST, "ones_c", [128, 1], BF16, 2304)
    ones_rb = place(ST, "ones_rb", [1, 128], BF16, 2336)
    ones_rh = place(ST, "ones_rh", [1, 128], FP16, 2592)
    rowb = place(ST, "rowb", [1, D], BF16, 2848)
    rowh = place(ST, "rowh", [1, D], FP16, 2848)
    bq_sb = place(ST, "bq_sb", [128, 8], F32, 4896)
    bk_sb = place(ST, "bk_sb", [128, 8], F32, 4928)
    b1_sb = place(ST, "b1_sb", [128, 32], F32, 4960)
    css = place(ST, "css", [48, 16], F32, 5088)
    k1q1 = place(ST, "k1q1", [128, 16], F32, 5152)
    acc = place(ST, "acc", [128, 16], F32, 5216)
    stat = place(ST, "stat", [128, 16], F32, 5280)
    bd = place(ST, "bd", [128, 308], BF16, 5344)
    lneps_c = place(ST, "lneps_c", [128, 1], F32, 5960)
    mhalf_c = place(ST, "mhalf_c", [128, 1], F32, 5964)
    mone_c = place(ST, "mone_c", [128, 1], F32, 5968)

    xstg = [place(KT, f"xstg{i}", [128, 2048], BF16, 57344 + i * 4096) for i in range(2)]
    wqs = [place(POOL, f"wqs{i}", [128, 1024], BF16, i * 4096) for i in range(2)]
    wks = [place(POOL, f"wks{i}", [128, 1024], BF16, 2048 + i * 4096) for i in range(2)]
    si_sm = place(POOL, "si_sm", [16, L], BF16, 0)
    so_sm = place(POOL, "so_sm", [48, L], BF16, 0)
    rsc = place(POOL, "rsc", [48, 512], BF16, 8192)
    disc2 = [place(POOL, "disc0", [128, 512], BF16, 9216),
             place(POOL, "disc1", [128, 512], BF16, 8192)]
    sctok = place(POOL, "sctok", [128, 512], BF16, 9216)
    Asb = place(POOL, "Asb", [128, 1024], BF16, 0)
    ATsb = place(POOL, "ATsb", [128, 1024], BF16, 2048)
    wvs = [place(POOL, f"wvs{i}", [128, 1024], BF16, 4096 + i * 2048) for i in range(2)]
    kvsb = place(POOL, "kvsb", [128, 1024], BF16, 8192)
    kscT = place(ST, "kscT", [1, 128], BF16, 5344)
    r1b2 = [place(POOL, f"r1b{i}", [128, 1024], BF16, i * 2048) for i in range(2)]
    sqs = place(POOL, "sqs", [128, 1024], FP8, 4096)
    w1cmb = [place(POOL, f"w1cmb{i}", [128, 1024], BF16, i * 2048) for i in range(2)]
    sqs2 = place(POOL, "sqs2", [128, 1024], FP8, 4096)
    outs = place(POOL, "outs", [128, 1024], F32, 5120)
    f16h = place(POOL, "f16h", [128, 512], FP16, 9216)

    w2q = place(QT, "w2q", [128, 32768], FP8, 0)
    h_sb = place(QT, "h_sb", [128, 32768], FP8, 32768)
    wo_sb = place(QT, "wo_sb", [128, 8192], BF16, 32768)

    pA = pst("pA", [128, 1024], F32)
    pB = pst("pB", [128, 1024], F32)
    pC = pst("pC", [128, 512], F32)
    pCx = pst("pCx", [128, 512], F32)
    pT = pst("pT", [128, 1024], BF16)
    pT2 = pst("pT2", [128, 1024], BF16)

    em.make_sems(stack)
    T, V, S, G, Y = ENGINES

    XT4 = XT[:, :].rearrange("p (kc lt j) -> p kc lt j", kc=CH, lt=NT)
    XTq = XT[:, :].rearrange("p (lt kc j) -> p lt kc j", lt=NT, kc=CH)
    KT4 = KT[:, :].rearrange("p (kc lt j) -> p kc lt j", kc=CH, lt=NT)
    pTr = pT[:, :].rearrange("p (kc j) -> p kc j", kc=CH)
    pT2r = pT2[:, :].rearrange("p (kc j) -> p kc j", kc=CH)

    def ck(name):
        if stop == name:
            raise _StopEmission()

    tens = {}

    def _emit_all():
        tens.update(dict(XT=(XT, 0, 128), QT=(QT, 0, 128), KT=(KT, 0, 128),
                         k1q1=(k1q1, 0, 128), css_lo=(css, 0, 16), css_hi=(css, 32, 48),
                         sctok=(sctok, 0, 128), kscT=(kscT, 0, 1), kvsb=(kvsb, 0, 128),
                         si=(si_sm, 0, 16), so=(so_sm, 32, 48), bd=(bd, 0, 128),
                         stat=(stat, 0, 128), h=(h_sb, 0, 128)))
        def gseq(fn):
            t = em.op(G, fn, inc="G")
            em.wait(G, t)
            return t
        gseq(lambda e: e.memset(ident[:, :], 0.0))
        gseq(lambda e: e.affine_select(
            out=ident[:, :], in_=ident[:, :], compare_op=ALU.not_equal,
            fill=1.0, base=0, pattern=[[-1, 128]], channel_multiplier=1))
        for ro in (0, 32):
            reg = selbc[ro:ro + 16, :]
            gseq(lambda e, reg=reg: e.memset(reg, 1.0))
            gseq(lambda e, reg=reg: e.affine_select(
                out=reg, in_=reg, compare_op=ALU.is_ge, fill=0.0,
                base=0, pattern=[[1, 1024]], channel_multiplier=-64))
            gseq(lambda e, reg=reg: e.affine_select(
                out=reg, in_=reg, compare_op=ALU.is_ge, fill=0.0,
                base=63, pattern=[[-1, 1024]], channel_multiplier=64))
        gseq(lambda e: e.memset(ones_c[:, :], 1.0))
        gseq(lambda e: e.memset(lneps_c[:, :], LNEPS))
        gseq(lambda e: e.memset(mhalf_c[:, :], -0.5))
        gseq(lambda e: e.memset(mone_c[:, :], -1.0))
        em.op(V, lambda e: e.memset(ones_rb[:, :], 1.0))
        em.op(V, lambda e: e.memset(ones_rh[:, :], 1.0))
        em.op(V, lambda e: e.memset(k1q1[:, :], 0.0))
        em.op(V, lambda e: e.memset(css[:, :], 0.0))
        em.op(V, lambda e: e.memset(acc[:, :], 0.0))
        em.op(V, lambda e: e.memset(stat[:, :], 0.0))
        t_statics_v = em.mark(V)
        em.wait(G, t_statics_v)
        t_statics = em.mark(G)

        t_bq = em.dma(Y, bq_sb[:, :], bq_d[:, :], "DY")
        t_bk = em.dma(Y, bk_sb[:, :], bk_d[:, :], "DY")
        t_b1 = em.dma(Y, b1_sb[:, :], b1_d[:, :], "DY")
        t_bv = em.dma(G, rowb[:, :], bv_d[:, :], "DG")

        stg_free = [None, None]
        pt_free = [None, None]
        xT_ready = []
        wpre = {}
        for mb in range(2):
            wpre[mb] = (em.dma(G, wqs[mb][:, :], wq_d[:, mb * 1024:(mb + 1) * 1024], "DG"),
                        em.dma(G, wks[mb][:, :], wk_d[:, mb * 1024:(mb + 1) * 1024], "DG"))
        x_pair = x_d.rearrange("(m two p) d -> m p two d", two=2, p=128)
        for mt in range(NT // 2):
            par = mt % 2
            em.wait(G, stg_free[par])
            t_ld = em.dma(G, xstg[par][:, :].rearrange("p (two d) -> p two d", two=2),
                          x_pair[mt], "DG")
            em.wait(T, t_ld)
            em.wait(T, t_statics_v)
            em.wait(T, t_statics)
            tok = None
            for two in range(2):
                lt = 2 * mt + two
                p2 = lt % 2
                pt = pT if p2 == 0 else pT2
                em.wait(T, pt_free[p2])
                for kc in range(CH):
                    tok = em.op(T, lambda e, kc=kc, two=two, par=par, pt=pt: e.transpose(
                        pt[:, kc * 128:(kc + 1) * 128],
                        xstg[par][:, two * 1024 + kc * 128: two * 1024 + (kc + 1) * 128],
                        ident[:, :]),
                        inc="T" if kc == CH - 1 else None)
                if two == 1:
                    stg_free[par] = tok
                eng = V if p2 == 0 else S
                em.wait(eng, tok)
                ptr = pTr if p2 == 0 else pT2r
                dst = XT4[:, :, lt, :]
                if eng == V:
                    t_ev = em.op(V, lambda e, ptr=ptr, dst=dst: e.tensor_copy(dst, ptr), inc="V", drain=False)
                else:
                    t_ev = em.op(S, lambda e, ptr=ptr, dst=dst: e.copy(dst, ptr), inc="S", drain=False)
                pt_free[p2] = t_ev
                xT_ready.append(t_ev)

        ck("P")
        w_free = [None, None]
        ev_per = {0: [], 1: []}
        for mb in range(CH):
            par = mb % 2
            if mb < 2:
                t_wq, t_wk = wpre[mb]
            else:
                em.wait(G, w_free[par])
                t_wq = em.dma(G, wqs[par][:, :], wq_d[:, mb * 1024:(mb + 1) * 1024], "DG")
                t_wk = em.dma(G, wks[par][:, :], wk_d[:, mb * 1024:(mb + 1) * 1024], "DG")
            last_k = None
            em.wait(S, t_bq)
            em.wait(S, t_bk)
            specs = ((wqs[par], t_wq, QT, pA, bq_sb, 0), (wks[par], t_wk, KT, bk_sb and pB, bk_sb, 8))
            specs = ((wqs[par], t_wq, QT, pA, bq_sb, 0), (wks[par], t_wk, KT, pB, bk_sb, 8))

            def emit_unit(wi, n):
                wsl, wtok, dst, psum, bias_sb, accbase = specs[wi]
                bank = psum[:, (n % 2) * 512:(n % 2) * 512 + 512]
                em.wait(T, wtok)
                idx = len(ev_per[wi])
                if idx >= 2:
                    em.wait(T, ev_per[wi][idx - 2])
                em.wait_final(T, xT_ready[4 * n:4 * n + 4])
                last = None
                for kc in range(CH):
                    last = em.op(T, lambda e, wsl=wsl, kc=kc, n=n, bank=bank: e.matmul(
                        bank, wsl[:, kc * 128:(kc + 1) * 128],
                        XT[:, kc * 4096 + n * 512: kc * 4096 + n * 512 + 512],
                        start=(kc == 0), stop=(kc == CH - 1)),
                        inc="T" if kc == CH - 1 else None)
                em.wait(S, last)
                if dst is KT and mb == CH - 1:
                    em.wait(S, stg_free[0])
                    em.wait(S, stg_free[1])
                t_ev = em.op(S, lambda e, bank=bank, dst=dst, mb=mb, n=n, bias_sb=bias_sb, accbase=accbase: e.activation(
                    dst[:, mb * 4096 + n * 512: mb * 4096 + n * 512 + 512],
                    bank, AF.Sigmoid, bias=bias_sb[:, mb:mb + 1],
                    accum_out=acc[:, accbase + n:accbase + n + 1]), inc="S", drain=False)
                ev_per[wi].append(t_ev)
                return last

            if mb < 2:
                for n in range(8):
                    for wi in range(2):
                        last_k = emit_unit(wi, n)
            else:
                for wi in range(2):
                    for n in range(8):
                        last_k = emit_unit(wi, n)
            for wi in range(2):
                _, _, dst, _, _, accbase = specs[wi]
                em.wait(V, ev_per[wi][-1])
                ccol = (8 + mb) if dst is QT else mb
                t_red = em.op(V, lambda e, accbase=accbase, ccol=ccol: e.reduce_sum(
                    k1q1[:, ccol:ccol + 1], acc[:, accbase:accbase + 8], axis=AX.X), inc="V")
                em.wait(S, t_red)
            w_free[par] = last_k
        em.barrier()

        xtok_ready = []
        for lt in range(NT):
            t = em.dma(G, XT[:, lt * 1024:(lt + 1) * 1024], x_d[lt * 128:(lt + 1) * 128, :], "DG")
            xtok_ready.append(t)

        ck("A")
        def bd_tile(base, mb):
            return bd[:, base + 18 * mb: base + 18 * mb + 16]

        def build_bd(colA, colB):
            em.op(V, lambda e: e.memset(bd[:, :], 0.0))
            for base, col in ((0, colA), (144, colB)):
                ev = bd[0:64, base:base + 160].rearrange("p (a b) -> p a b", a=8, b=20)[:, :, 0:1]
                od = bd[64:128, base + 1:base + 161].rearrange("p (a b) -> p a b", a=8, b=20)[:, :, 0:1]
                em.op(V, lambda e, ev=ev, col=col: e.tensor_scalar_add(
                    ev, k1q1[0:64, col:col + 8].rearrange("p (a b) -> p a b", b=1), EPS))
                em.op(V, lambda e, od=od, col=col: e.tensor_scalar_add(
                    od, k1q1[64:128, col:col + 8].rearrange("p (a b) -> p a b", b=1), EPS))
            return em.mark(V)

        t_bd = build_bd(0, 8)

        em.wait(T, t_bd)
        prev_cev = None
        for cidx, (bdc, ro) in enumerate(((0, 0), (144, 32))):
            em.wait(T, prev_cev)
            last = None
            for mb in range(CH):
                last = em.op(T, lambda e, mb=mb, bdc=bdc, ro=ro: e.matmul(
                    pC[ro:ro + 16, 0:1], bd_tile(bdc, mb),
                    ones_c[:, :],
                    start=(mb == 0), stop=(mb == CH - 1)), inc="T" if mb == CH - 1 else None)
            em.wait(V, last)
            prev_cev = em.op(V, lambda e, ro=ro, cidx=cidx: e.tensor_scalar_mul(
                css[ro:ro + 16, cidx:cidx + 1], pC[ro:ro + 16, 0:1], EPS), inc="V")
        t_c12 = em.mark(V)
        em.wait(T, t_c12)

        prev_rec = {0: None, 1: None}
        sis_tok = {0: [None] * 8, 1: [None] * 8}
        for which, (src, bdc, ro, dst) in enumerate(((QT, 0, 0, si_sm), (KT, 144, 32, so_sm))):
            for n in range(8):
                pc = pC if n % 2 == 0 else pCx
                em.wait(T, prev_rec[n % 2])
                last = None
                for mb in range(CH):
                    last = em.op(T, lambda e, mb=mb, n=n, src=src, bdc=bdc, ro=ro, pc=pc: e.matmul(
                        pc[ro:ro + 16, 0:512], bd_tile(bdc, mb),
                        src[:, mb * 4096 + n * 512: mb * 4096 + n * 512 + 512],
                        start=(mb == 0), stop=(mb == CH - 1)), inc="T" if mb == CH - 1 else None)
                em.wait(S, last)
                em.op(S, lambda e, pc=pc, ro=ro, which=which: e.activation(
                    pc[ro:ro + 16, 0:512], pc[ro:ro + 16, 0:512], AF.Ln,
                    bias=css[ro:ro + 16, which:which + 1]))
                t_r = em.op(S, lambda e, dst=dst, pc=pc, ro=ro, n=n: e.activation(
                    dst[ro:ro + 16, n * 512:(n + 1) * 512], pc[ro:ro + 16, 0:512],
                    AF.Exp, scale=mone_c[ro:ro + 16, 0:1]), inc="S")
                prev_rec[n % 2] = t_r
                sis_tok[which][n] = t_r

        ck("B")
        slots = [pC[:, 0:512], pCx[:, 0:512], pA[:, 0:512], pA[:, 512:1024],
                 pB[:, 0:512], pB[:, 512:1024]]
        slot_free = [prev_rec[0], prev_rec[1]] + [None] * 4
        i_s = 0
        for which, (src, smrow, ro, outbase) in enumerate(((QT, si_sm, 0, 0), (KT, so_sm, 32, 8))):
            for mb in range(CH):
                last_p = None
                for n in range(8):
                    sl = i_s % 6
                    i_s += 1
                    pc = slots[sl]
                    em.wait(T, slot_free[sl])
                    em.wait(T, sis_tok[which][n])
                    t_b = em.op(T, lambda e, mb=mb, n=n, smrow=smrow, ro=ro, pc=pc: e.matmul(
                        pc, selbc[ro:ro + 16, mb * 128:(mb + 1) * 128],
                        smrow[ro:ro + 16, n * 512:(n + 1) * 512],
                        start=True, stop=True), inc="T")
                    em.wait(V, t_b)
                    t_p = em.op(V, lambda e, src=src, mb=mb, n=n, pc=pc, dd=disc2[i_s % 2]: e.scalar_tensor_tensor(
                        dd[:, :], src[:, mb * 4096 + n * 512: mb * 4096 + n * 512 + 512],
                        1.0, pc, ALU.mult, ALU.mult,
                        accum_out=acc[:, n:n + 1]), inc="V", drain=False)
                    slot_free[sl] = t_p
                    last_p = t_p
                em.wait(V, last_p)
                em.op(V, lambda e, outbase=outbase, mb=mb: e.reduce_sum(
                    k1q1[:, outbase + mb:outbase + mb + 1], acc[:, 0:8], axis=AX.X), inc="V")
        t_qsikso = em.mark(V)

        em.wait(V, t_qsikso)
        t_bd2 = build_bd(8, 0)
        em.wait(T, t_bd2)
        prev_cev = None
        for cidx, (bdc, ro) in enumerate(((0, 0), (144, 32))):
            em.wait(T, prev_cev)
            last = None
            for mb in range(CH):
                last = em.op(T, lambda e, mb=mb, bdc=bdc, ro=ro: e.matmul(
                    pC[ro:ro + 16, 0:1], bd_tile(bdc, mb),
                    ones_c[:, :],
                    start=(mb == 0), stop=(mb == CH - 1)), inc="T" if mb == CH - 1 else None)
            em.wait(V, last)
            prev_cev = em.op(V, lambda e, ro=ro, cidx=cidx: e.tensor_scalar_mul(
                css[ro:ro + 16, 2 + cidx:3 + cidx], pC[ro:ro + 16, 0:1], EPS), inc="V")
        t_c34 = em.mark(V)
        em.wait(T, t_c34)
        ck("C")

        sink_slots = [pC[:, 0:512], pCx[:, 0:512], pA[:, 0:512]]
        srcq_slots = [pA[:, 512:1024], pB[:, 0:512], pB[:, 512:1024]]
        sink_free = [None] * 3
        srcq_free = [None] * 3
        rsc_free = [None, None]
        prev_clip = None
        for n in range(8):
            sl = n % 3
            pc = sink_slots[sl]
            em.wait(T, sink_free[sl])
            last = None
            for mb in range(CH):
                last = em.op(T, lambda e, mb=mb, n=n, pc=pc: e.matmul(
                    pc[0:16, :], bd_tile(0, mb),
                    QT[:, mb * 4096 + n * 512: mb * 4096 + n * 512 + 512],
                    start=(mb == 0), stop=(mb == CH - 1)), inc="T" if mb == CH - 1 else None)
            em.wait(S, last)
            em.wait(S, rsc_free[n % 2])
            sa_buf = rsc if n % 2 == 0 else disc2[0]
            t_sa = em.op(S, lambda e, pc=pc, sa_buf=sa_buf: e.activation(
                sa_buf[0:16, :], pc[0:16, :], AF.Sigmoid, bias=css[0:16, 2:3]), inc="S")
            em.wait(V, t_sa)
            t_sisa = em.op(V, lambda e, n=n, sa_buf=sa_buf: e.tensor_mul(
                si_sm[0:16, n * 512:(n + 1) * 512], si_sm[0:16, n * 512:(n + 1) * 512],
                sa_buf[0:16, :]), inc="V")
            rsc_free[n % 2] = t_sisa
            sink_free[sl] = t_sisa
            sl2 = n % 3
            pcx = srcq_slots[sl2]
            em.wait(T, srcq_free[sl2])
            last2 = None
            for mb in range(CH):
                last2 = em.op(T, lambda e, mb=mb, n=n, pcx=pcx: e.matmul(
                    pcx[32:48, :], bd_tile(144, mb),
                    KT[:, mb * 4096 + n * 512: mb * 4096 + n * 512 + 512],
                    start=(mb == 0), stop=(mb == CH - 1)), inc="T" if mb == CH - 1 else None)
            em.wait(V, last2)
            em.op(V, lambda e, pcx=pcx, n=n: e.tensor_scalar(
                so_sm[32:48, n * 512:(n + 1) * 512], pcx[32:48, :],
                css[32:48, 3:4], 1.0, ALU.add, ALU.min))
            t_clip = em.op(V, lambda e, n=n: e.tensor_scalar_max(
                so_sm[32:48, n * 512:(n + 1) * 512],
                so_sm[32:48, n * 512:(n + 1) * 512], -1.0), inc="V")
            srcq_free[sl2] = t_clip
            prev_clip = t_clip

        em.wait(S, prev_clip)
        t_exp = em.op(S, lambda e: e.activation(
            so_sm[32:48, :], so_sm[32:48, :], AF.Exp,
            accum_out=css[32:48, 8:9]), inc="S")
        em.wait(V, t_exp)
        em.op(V, lambda e: e.reciprocal(css[32:48, 9:10], css[32:48, 8:9]))
        t_sc = em.op(V, lambda e: e.tensor_scalar(
            so_sm[32:48, :], so_sm[32:48, :], css[32:48, 9:10], float(L),
            ALU.mult, ALU.mult), inc="V")

        em.wait(T, t_sc)
        prev_ev = {0: None, 1: None}
        for lt in range(NT):
            par = lt % 2
            pt = pT if par == 0 else pT2
            em.wait(T, prev_ev[par])
            t_tr = em.op(T, lambda e, lt=lt, pt=pt: e.transpose(
                pt[:, 0:16], so_sm[32:48, lt * 128:(lt + 1) * 128],
                ident[32:48, 32:48]), inc="T")
            em.wait(V, t_tr)
            t_ev = em.op(V, lambda e, lt=lt, pt=pt: e.tensor_copy(
                sctok[:, lt * 16:(lt + 1) * 16], pt[:, 0:16]), inc="V")
            prev_ev[par] = t_ev

        ck("D")
        sctok_r = sctok[:, :].rearrange("p (lt h) -> p lt h", lt=NT)
        for mb in range(CH):
            for lt4 in range(NT // 4):
                i = mb * (NT // 4) + lt4
                par = i % 2
                pt = pT if par == 0 else pT2
                lt = lt4 * 4
                em.wait(T, prev_ev[par])
                t_tr = None
                for j in range(4):
                    t_tr = em.op(T, lambda e, mb=mb, lt=lt, j=j, pt=pt: e.transpose(
                        pt[:, j * 128:(j + 1) * 128],
                        KT[:, mb * 4096 + (lt + j) * 128: mb * 4096 + (lt + j) * 128 + 128],
                        ident[:, :]), inc="T" if j == 3 else None)
                em.wait(V, t_tr)
                scb = sctok_r[:, lt:lt + 4, 2 * mb:2 * mb + 2].rearrange(
                    "p b h -> p b h 1" if False else "p (b one) h -> p b one h", one=1
                )
                scb = sctok_r[:, lt:lt + 4, 2 * mb:2 * mb + 2].unsqueeze(3).broadcast_to([128, 4, 2, 64])
                t_ev = em.op(V, lambda e, mb=mb, lt=lt, pt=pt, scb=scb: e.tensor_tensor(
                    KT[:, mb * 4096 + lt * 128: mb * 4096 + lt * 128 + 512].rearrange("p (b h j) -> p b h j", b=4, h=2),
                    pt[:, 0:512].rearrange("p (b h j) -> p b h j", b=4, h=2), scb, ALU.mult), inc="V", drain=False)
                prev_ev[par] = t_ev

        ck("C3")
        slots = [pC[:, 0:512], pCx[:, 0:512], pA[:, 0:512], pA[:, 512:1024],
                 pB[:, 0:512], pB[:, 512:1024]]
        slot_free = [None] * 6
        i_s = 0
        for mb in range(CH):
            for n in range(8):
                sl = i_s % 6
                i_s += 1
                pc = slots[sl]
                em.wait(T, slot_free[sl])
                t_b = em.op(T, lambda e, mb=mb, n=n, pc=pc: e.matmul(
                    pc, selbc[0:16, mb * 128:(mb + 1) * 128],
                    si_sm[0:16, n * 512:(n + 1) * 512], start=True, stop=True), inc="T")
                em.wait(V, t_b)
                t_e = em.op(V, lambda e, mb=mb, n=n, pc=pc: e.tensor_mul(
                    QT[:, mb * 4096 + n * 512: mb * 4096 + n * 512 + 512],
                    QT[:, mb * 4096 + n * 512: mb * 4096 + n * 512 + 512],
                    pc), inc="V", drain=False)
                slot_free[sl] = t_e
        em.barrier()

        ck("E")
        em.wait_final(T, xtok_ready)
        t_kvz = em.op(G, lambda e: e.memset(kvsb[:, :], 0.0), inc="G")
        fstate = dict(t_wv_free={}, pa_free=[None, None], prev_kr=None,
                      prev_AT=None, prev_kv_mm=None, prev_kvev=None,
                      wv_tok={}, amm_tok={})

        def emit_A_mms(c):
            par = c % 2
            pa = pA if par == 0 else pB
            em.wait(G, fstate["t_wv_free"].get(c % 2))
            pf = fstate["pa_free"][c % 2]
            fstate["wv_tok"][c] = em.dma(G, wvs[c % 2][:, :], wv_d[:, c * 1024:(c + 1) * 1024], "DG")
            em.wait_many(T, pf if isinstance(pf, list) else [pf])
            last = None
            for lt in range(NT):
                for half in range(2):
                    last = em.op(T, lambda e, c=c, lt=lt, half=half, pa=pa: e.matmul(
                        pa[:, half * 512: half * 512 + 512],
                        KT[:, c * 4096 + lt * 128: c * 4096 + lt * 128 + 128],
                        XT[:, lt * 1024 + half * 512: lt * 1024 + half * 512 + 512],
                        start=(lt == 0), stop=(lt == NT - 1)),
                        inc="T" if (lt == NT - 1 and half == 1) else None)
            fstate["amm_tok"][c] = last

        def emit_F_tail(c):
            par = c % 2
            pa = pA if par == 0 else pB
            last = fstate["amm_tok"][c]
            prow = pCx if c < 4 else pC
            em.wait(T, fstate["prev_kr"])
            lastr = None
            for lt in range(NT):
                lastr = em.op(T, lambda e, c=c, lt=lt, prow=prow: e.matmul(
                    prow[0:1, 128:256], ones_c[:, :],
                    KT[:, c * 4096 + lt * 128: c * 4096 + lt * 128 + 128],
                    start=(lt == 0), stop=(lt == NT - 1)), inc="T" if lt == NT - 1 else None)
            em.wait(V, lastr)
            em.wait(V, fstate["prev_kv_mm"])
            t_kr = em.op(V, lambda e, prow=prow: e.tensor_copy(
                kscT[0:1, :], prow[0:1, 128:256]), inc="V")
            fstate["prev_kr"] = t_kr
            em.wait(V, last)
            em.wait(V, fstate["prev_AT"])
            t_A = em.op(V, lambda e, pa=pa: e.tensor_copy(Asb[:, :], pa[:, :]), inc="V")
            fstate["pa_free"][par] = t_A
            em.wait(T, t_A)
            em.wait(V, fstate["prev_kv_mm"])
            tr_toks = []
            for jc in range(CH):
                par2 = jc % 2
                pt = pT if par2 == 0 else pT2
                if jc >= 2:
                    em.wait(T, tr_toks[jc - 2])
                t_tr = em.op(T, lambda e, jc=jc, pt=pt: e.transpose(
                    pt[:, 0:128], Asb[:, jc * 128:(jc + 1) * 128], ident[:, :]), inc="T")
                em.wait(V, t_tr)
                t_evt = em.op(V, lambda e, jc=jc, pt=pt: e.tensor_copy(
                    ATsb[:, jc * 128:(jc + 1) * 128], pt[:, 0:128]), inc="V", drain=False)
                tr_toks.append(t_evt)
            fstate["prev_AT"] = tr_toks[-1]
            pkv = pC if c < 4 else pCx
            em.wait_many(T, [tr_toks[-1], fstate["wv_tok"][c], t_kr, fstate["prev_kvev"], t_bv])
            for jc in range(CH):
                em.op(T, lambda e, c=c, jc=jc, pkv=pkv: e.matmul(
                    pkv[:, (c % 4) * 128:(c % 4) * 128 + 128],
                    ATsb[:, jc * 128:(jc + 1) * 128],
                    wvs[c % 2][:, jc * 128:(jc + 1) * 128],
                    start=(jc == 0), stop=False))
            t_kv = em.op(T, lambda e, c=c, pkv=pkv: e.matmul(
                pkv[:, (c % 4) * 128:(c % 4) * 128 + 128],
                kscT[0:1, :],
                rowb[0:1, c * 128:(c + 1) * 128],
                start=False, stop=True), inc="T")
            fstate["t_wv_free"][c % 2] = t_kv
            fstate["prev_kv_mm"] = t_kv
            em.wait(V, t_kvz)
            em.wait(V, t_kv)
            em.op(V, lambda e, c=c, pkv=pkv: e.tensor_copy(
                kvsb[0:64, c * 128: c * 128 + 64],
                pkv[0:64, (c % 4) * 128:(c % 4) * 128 + 64]))
            t_kvev = em.op(V, lambda e, c=c, pkv=pkv: e.tensor_copy(
                kvsb[64:128, c * 128 + 64: c * 128 + 128],
                pkv[64:128, (c % 4) * 128 + 64:(c % 4) * 128 + 128]), inc="V")
            fstate["prev_kvev"] = t_kvev

        wo_w2 = {}

        def emit_G(c):
            par = c % 2
            pa = pA if par == 0 else pB
            slots = [pa[:, 0:512], pa[:, 512:1024]]
            pf = fstate["pa_free"][par]
            pf = pf if isinstance(pf, list) else [pf, pf]
            free = [pf[0], pf[1]]
            for n in range(8):
                sl = n % 2
                em.wait(T, free[sl])
                em.wait(T, fstate["prev_kvev"])
                t_mm = em.op(T, lambda e, c=c, n=n, sl=sl, slots=slots: e.matmul(
                    slots[sl], kvsb[:, c * 128:(c + 1) * 128],
                    QT[:, c * 4096 + n * 512: c * 4096 + n * 512 + 512],
                    start=True, stop=True), inc="T")
                eng = V if n % 2 == 0 else S
                em.wait(eng, t_mm)
                if eng == V:
                    t_ev = em.op(V, lambda e, c=c, n=n, sl=sl, slots=slots: e.tensor_copy(
                        KT[:, c * 4096 + n * 512: c * 4096 + n * 512 + 512], slots[sl]), inc="V", drain=False)
                else:
                    t_ev = em.op(S, lambda e, c=c, n=n, sl=sl, slots=slots: e.copy(
                        KT[:, c * 4096 + n * 512: c * 4096 + n * 512 + 512], slots[sl]), inc="S", drain=False)
                free[sl] = t_ev
            fstate["pa_free"][par] = [free[0], free[1]]
            if c == 3:
                em.wait(G, free[0])
                em.wait(G, free[1])
                wo_w2["w2"] = em.dma(G, w2q[:, :], w2_d[:, :], "DG")
            if c == 5:
                em.wait(G, free[0])
                em.wait(G, free[1])
                wo_w2["wo"] = em.dma(G, wo_sb[:, :], wo_d[:, :], "DG")

        emit_A_mms(0)
        for c in range(1, CH):
            emit_A_mms(c)
            emit_F_tail(c - 1)
            emit_G(c - 1)
        emit_F_tail(CH - 1)
        emit_G(CH - 1)
        em.barrier()

        t_bo = em.dma(G, rowb[:, :], bo_d[:, :], "DG")

        ck("F")

        ck("G")
        em.wait(T, wo_w2["wo"])
        em.wait(T, t_bo)
        pa_free = [None, None]
        pt_free = [None, None]
        prev_norm = None
        r1_free = [None, None]
        pend_tr = []

        def emit_y1_transpose(lt, t_y1):
            par = lt % 2
            em.wait(T, t_y1)
            em.wait(T, pt_free[par])
            pt = pT if par == 0 else pT2
            tok = None
            for kc in range(CH):
                tok = em.op(T, lambda e, kc=kc, lt=lt, pt=pt: e.transpose(
                    pt[:, kc * 128:(kc + 1) * 128],
                    KT[:, kc * 4096 + lt * 128: kc * 4096 + lt * 128 + 128],
                    ident[:, :]), inc="T" if kc == CH - 1 else None)
            eng = V if par == 0 else S
            em.wait(eng, tok)
            ptr = pTr if par == 0 else pT2r
            dstT = XT[:, lt * 1024:(lt + 1) * 1024].rearrange("p (kc j) -> p kc j", kc=CH)
            if eng == V:
                t_evT = em.op(V, lambda e, ptr=ptr, dstT=dstT: e.tensor_copy(dstT, ptr), inc="V", drain=False)
            else:
                t_evT = em.op(S, lambda e, ptr=ptr, dstT=dstT: e.copy(dstT, ptr), inc="S", drain=False)
            pt_free[par] = t_evT

        for lt in range(NT):
            par = lt % 2
            pa = pA if par == 0 else pB
            em.wait(T, pa_free[par])
            last = None
            for c in range(CH):
                for half in range(2):
                    last = em.op(T, lambda e, c=c, lt=lt, half=half, pa=pa: e.matmul(
                        pa[:, half * 512: half * 512 + 512],
                        KT[:, c * 4096 + lt * 128: c * 4096 + lt * 128 + 128],
                        wo_sb[:, c * 1024 + half * 512: c * 1024 + half * 512 + 512],
                        start=(c == 0), stop=False),
                        inc=None)
            for half in range(2):
                last = em.op(T, lambda e, half=half, pa=pa: e.matmul(
                    pa[:, half * 512: half * 512 + 512],
                    ones_rb[:, :], rowb[0:1, half * 512: half * 512 + 512],
                    start=False, stop=True), inc="T" if half == 1 else None)
            em.wait(V, last)
            em.wait(V, r1_free[par])
            r1b = r1b2[par]
            t_r1 = em.op(V, lambda e, pa=pa, lt=lt, r1b=r1b: e.scalar_tensor_tensor(
                r1b[:, :], pa[:, :], 1.0, XT[:, lt * 1024:(lt + 1) * 1024],
                ALU.mult, ALU.add, accum_out=stat[:, 0:1]), inc="V")
            pa_free[par] = t_r1
            em.wait(S, t_r1)
            em.wait(S, prev_norm)
            t_sq = em.op(S, lambda e, r1b=r1b: e.activation(
                sqs[:, :], r1b[:, :], AF.Square, accum_out=stat[:, 1:2]), inc="S")
            em.wait(V, t_sq)
            em.op(V, lambda e: e.tensor_scalar_mul(stat[:, 2:3], stat[:, 0:1], 1.0 / D))
            em.op(V, lambda e: e.tensor_mul(stat[:, 4:5], stat[:, 2:3], stat[:, 2:3]))
            t_var = em.op(V, lambda e: e.scalar_tensor_tensor(
                stat[:, 3:4], stat[:, 1:2], 1.0 / D, stat[:, 4:5],
                ALU.mult, ALU.subtract), inc="V")
            em.wait(S, t_var)
            em.op(S, lambda e: e.activation(stat[:, 5:6], stat[:, 3:4], AF.Ln, bias=lneps_c[:, 0:1]))
            t_rstd = em.op(S, lambda e: e.activation(stat[:, 6:7], stat[:, 5:6], AF.Exp, scale=mhalf_c[:, 0:1]), inc="S")
            em.wait(V, t_rstd)
            em.wait(V, prev_norm)
            t_nmr = em.op(V, lambda e: e.scalar_tensor_tensor(
                stat[:, 7:8], stat[:, 2:3], -1.0, stat[:, 6:7], ALU.mult, ALU.mult), inc="V")
            em.wait(S, t_nmr)
            t_y1 = em.op(S, lambda e, lt=lt, r1b=r1b: e.activation(
                KT4[:, :, lt, :], r1b[:, :], AF.Identity,
                bias=stat[:, 7:8], scale=stat[:, 6:7]), inc="S")
            prev_norm = t_y1
            r1_free[par] = t_y1
            pend_tr.append((lt, t_y1))
            if len(pend_tr) > 1:
                emit_y1_transpose(*pend_tr.pop(0))
        while pend_tr:
            emit_y1_transpose(*pend_tr.pop(0))
        em.barrier()

        t_b2 = em.dma(G, rowh[:, :], b2_d[:, :], "DG")

        ck("H")
        em.wait(T, wo_w2["w2"])
        em.wait(T, t_b2)
        w1_free = [None, None]
        h_ev = []
        prev_f16 = None
        f16_par = [None, None]
        prev_store = None
        prev_out = None
        y_stores = []
        for g in range(4):
            h_ev = []
            for hc in range(HC):
                par = hc % 2
                pa = pB if par == 0 else pA
                wpar = hc % 2
                em.wait(G, w1_free[wpar])
                t_w1a = em.dma(G, w1cmb[wpar][:, :], w1_d[:, hc * 1024:(hc + 1) * 1024], "DG")
                if hc >= 2:
                    em.wait(T, h_ev[hc - 2])
                else:
                    em.wait(T, f16_par[0] if par == 0 else f16_par[1])
                last = None
                em.wait(T, t_w1a)
                for kc in range(CH):
                    for half in range(2):
                        rhs = XTq[:, g * 8 + half * 4: g * 8 + half * 4 + 4, kc, :]
                        want_inc = (kc == CH - 1 and half == 1)
                        tok = em.op(T, lambda e, kc=kc, half=half, pa=pa, wpar=wpar, rhs=rhs: e.matmul(
                            pa[:, half * 512: half * 512 + 512],
                            w1cmb[wpar][:, kc * 128:(kc + 1) * 128], rhs,
                            start=(kc == 0), stop=(kc == CH - 1)),
                            inc="T" if want_inc else None)
                        if kc == CH - 1 and half == 1:
                            last = tok
                w1_free[wpar] = last
                eng = V if par == 0 else S
                em.wait(eng, last)
                em.wait(eng, t_b1)
                if eng == V:
                    t_h = em.op(V, lambda e, hc=hc, pa=pa: e.tensor_scalar(
                        h_sb[:, hc * 1024:(hc + 1) * 1024], pa[:, :],
                        b1_sb[:, hc:hc + 1], 0.0, ALU.add, ALU.max), inc="V", drain=False)
                else:
                    t_h = em.op(S, lambda e, hc=hc, pa=pa: e.activation(
                        h_sb[:, hc * 1024:(hc + 1) * 1024], pa[:, :], AF.Relu,
                        bias=b1_sb[:, hc:hc + 1]), inc="S", drain=False)
                h_ev.append(t_h)
            em.wait_many(T, h_ev)
            for ls in range(8):
                lt = g * 8 + ls
                par = ls % 2
                pb = pB if par == 0 else pA
                em.wait(T, f16_par[par])
                last = None
                h4 = h_sb[:, :].rearrange("p (hc l) -> p hc l", hc=HC)
                w24 = w2q[:, :].rearrange("p (hc d) -> p hc d", hc=HC)
                for hc2 in range(HC // 2):
                    for half in range(2):
                        last = em.op(T, lambda e, hc2=hc2, ls=ls, half=half, pb=pb, h4=h4, w24=w24: e.matmul(
                            pb[:, half * 512: half * 512 + 512],
                            h4[:, 2 * hc2:2 * hc2 + 2, ls * 128: ls * 128 + 128],
                            w24[:, 2 * hc2:2 * hc2 + 2, half * 512: half * 512 + 512],
                            start=(hc2 == 0), stop=False,
                            perf_mode=mybir.MatmulPerfMode.DoubleRow), inc=None)
                for half in range(2):
                    last = em.op(T, lambda e, half=half, pb=pb: e.matmul(
                        pb[:, half * 512: half * 512 + 512],
                        ones_rh[:, :], rowh[0:1, half * 512: half * 512 + 512],
                        start=False, stop=True), inc="T" if half == 1 else None)
                em.wait(V, last)
                t_r2 = None
                for hf in range(2):
                    t_f16 = em.op(V, lambda e, pb=pb, hf=hf: e.tensor_copy(
                        f16h[:, :], pb[:, hf * 512:(hf + 1) * 512]), inc="V")
                    KT4h = KT4[:, 4 * hf:4 * hf + 4, lt, :]
                    t_r2 = em.op(V, lambda e, lt=lt, hf=hf, KT4h=KT4h: e.scalar_tensor_tensor(
                        KT4h, f16h[:, :].rearrange("p (c j) -> p c j", c=4),
                        1.0, KT4h, ALU.mult, ALU.add,
                        accum_out=acc[:, hf:hf + 1]), inc="V")
                    if hf == 1:
                        prev_f16 = t_r2
                        f16_par[par] = t_r2
                em.wait(V, prev_out)
                t_s1 = em.op(V, lambda e: e.tensor_add(stat[:, 8:9], acc[:, 0:1], acc[:, 1:2]), inc="V")
                em.wait(S, t_s1)
                em.wait(S, t_r2)
                t_sq = em.op(S, lambda e, lt=lt: e.activation(
                    sqs2[:, :], KT4[:, :, lt, :], AF.Square, accum_out=stat[:, 9:10]), inc="S")
                em.wait(V, t_sq)
                em.op(V, lambda e: e.tensor_scalar_mul(stat[:, 10:11], stat[:, 8:9], 1.0 / D))
                em.op(V, lambda e: e.tensor_mul(stat[:, 12:13], stat[:, 10:11], stat[:, 10:11]))
                t_var = em.op(V, lambda e: e.scalar_tensor_tensor(
                    stat[:, 11:12], stat[:, 9:10], 1.0 / D, stat[:, 12:13],
                    ALU.mult, ALU.subtract), inc="V")
                em.wait(S, t_var)
                em.op(S, lambda e: e.activation(stat[:, 13:14], stat[:, 11:12], AF.Ln, bias=lneps_c[:, 0:1]))
                t_rstd = em.op(S, lambda e: e.activation(stat[:, 14:15], stat[:, 13:14], AF.Exp, scale=mhalf_c[:, 0:1]), inc="S")
                em.wait(V, t_rstd)
                t_nmr = em.op(V, lambda e: e.scalar_tensor_tensor(
                    stat[:, 15:16], stat[:, 10:11], -1.0, stat[:, 14:15],
                    ALU.mult, ALU.mult), inc="V")
                em.wait(S, t_nmr)
                em.wait(S, prev_store)
                t_out = em.op(S, lambda e, lt=lt: e.activation(
                    outs[:, :], KT4[:, :, lt, :], AF.Identity,
                    bias=stat[:, 15:16], scale=stat[:, 14:15]), inc="S")
                prev_out = t_out
                em.wait(Y, t_out)
                t_st = em.dma(Y, y_d[lt * 128:(lt + 1) * 128, :], outs[:, :], "DY")
                y_stores.append(t_st)
                prev_store = t_st

        em.wait_final(Y, y_stores)


    try:
        _emit_all()
    except _StopEmission:
        pass
    if dump is not None:
        em.barrier()
        col = 0
        for name in dump:
            t, r0, r1 = tens[name]
            ncols = t.shape[1]
            nb = 1024
            for o in range(0, ncols, nb):
                w = min(nb, ncols - o, 32768 - col)
                if w <= 0:
                    break
                em.dma(G, dbg_d[0:r1 - r0, col:col + w], t[r0:r1, o:o + w], "DG")
                col += w
        for i in range(8):
            em.op(G, lambda e, i=i: e.wait_ge(em.sems[f"DG{i}"], em.cnt[f"DG{i}"]))

    with stack:
        with nc.allow_low_precision(reason="deliberate bf16 intermediates within tolerance"), nc.Block() as block:
            @block.tensor
            def _(e):
                for fn in em.prog["tensor"]:
                    fn(e)

            @block.vector
            def _(e):
                for fn in em.prog["vector"]:
                    e._skip_drain = False
                    fn(e)
                    if not getattr(e, "_skip_drain", False):
                        e.drain()

            @block.scalar
            def _(e):
                for fn in em.prog["scalar"]:
                    e._skip_drain = False
                    fn(e)
                    if not getattr(e, "_skip_drain", False):
                        e.drain()

            @block.gpsimd
            def _(e):
                for fn in em.prog["gpsimd"]:
                    fn(e)

            @block.sync
            def _(e):
                for fn in em.prog["sync"]:
                    fn(e)

    return nc


def _images(inputs):
    Wq = np.asarray(inputs["L0_attn_Wq"], np.float32)
    Wk = np.asarray(inputs["L0_attn_Wk"], np.float32)
    Wv = np.asarray(inputs["L0_attn_Wv"], np.float32)
    Wo = np.asarray(inputs["L0_attn_Wo"], np.float32)
    w1 = np.asarray(inputs["L0_ffn_w1"], np.float32)
    w2 = np.asarray(inputs["L0_ffn_w2"], np.float32)

    def img_mb(W):
        return np.ascontiguousarray(
            W.reshape(8, 128, 8, 128).transpose(1, 2, 0, 3).reshape(128, 8192))

    def img_kc(W):
        return np.ascontiguousarray(
            W.reshape(8, 128, 1024).transpose(1, 0, 2).reshape(128, 8192))

    return dict(
        wq=img_mb(Wq), wk=img_mb(Wk), wv=img_mb(Wv), wo=img_kc(Wo),
        w1=np.ascontiguousarray(
            w1.reshape(32, 128, 8, 128).transpose(3, 0, 2, 1).reshape(128, 32768)),
        w2=np.ascontiguousarray(
            w2.reshape(1024, 32, 128).transpose(2, 1, 0).reshape(128, 32768)),
        bq=np.ascontiguousarray(np.asarray(inputs["L0_attn_bq"], np.float32).reshape(8, 128).T),
        bk=np.ascontiguousarray(np.asarray(inputs["L0_attn_bk"], np.float32).reshape(8, 128).T),
        b1=np.ascontiguousarray(np.asarray(inputs["L0_ffn_b1"], np.float32).reshape(32, 128).T),
        bv=np.ascontiguousarray(np.asarray(inputs["L0_attn_bv"], np.float32)[None, :]),
        bo=np.ascontiguousarray(np.asarray(inputs["L0_attn_bo"], np.float32)[None, :]),
        b2=np.ascontiguousarray(np.asarray(inputs["L0_ffn_b2"], np.float32)[None, :]),
    )


def kernel(**inputs):
    x = np.asarray(inputs["L0_attn_x"], np.float32)
    imgs = _images(inputs)
    nc = build_program()
    in_maps = []
    for c in range(8):
        m = {"x": np.ascontiguousarray(x[c])}
        m.update(imgs)
        in_maps.append(m)
    res = run_bass_kernel_spmd(nc, in_maps, core_ids=list(range(8)))
    return np.stack([res.results[c]["y"] for c in range(8)], axis=0).astype(np.float32)
